# version 09
# baseline (speedup 1.0000x reference)
import math
import contextlib
import numpy as np
import ml_dtypes
import concourse.bass as bass
import concourse.mybir as mybir
from concourse.bass_utils import run_bass_kernel_spmd

F32 = mybir.dt.float32
BF16 = mybir.dt.bfloat16
I32 = mybir.dt.int32
AF = mybir.ActivationFunctionType
ALU = mybir.AluOpType
AX = mybir.AxisListType.X
BF = ml_dtypes.bfloat16

NC = 8
DBG = {"store": True, "rot": True, "rope": True, "mm": True, "cp": True, "rmm": True}


def make_cfg(S=8192, DM=4096, DFF=11008, VIS_LO=1, VIS_HI=513, KEEP=256,
             EXP_ACTIONS=56, pair_mode=True, qsec=1024):
    c = dict(S=S, DM=DM, DFF=DFF, VIS_LO=VIS_LO, VIS_HI=VIS_HI, KEEP=KEEP)
    c["H"] = DM // 128
    c["OWN"] = S // NC
    c["NG"] = c["OWN"] // 512
    c["HALF"] = S // 2 if pair_mode else S
    c["DC"] = DM // 128
    c["FC"] = DFF // 128
    c["MC"] = DM // 128
    c["IG"] = DM // 512
    c["NBLK"] = S // 128
    c["NVIS"] = VIS_HI - VIS_LO
    c["NVB"] = c["NVIS"] // 128
    c["TEXT_LO"] = VIS_HI
    c["TEXT_HI"] = max(VIS_HI, S - EXP_ACTIONS)
    c["M0"] = 40.0
    c["SCALE"] = 1.0 / math.sqrt(128.0)
    c["EPS"] = 1e-5
    c["pair_mode"] = pair_mode
    c["QSEC"] = min(qsec, c["HALF"])
    c["NQ"] = c["HALF"] // c["QSEC"]
    assert c["OWN"] % 512 == 0 and c["HALF"] % c["QSEC"] == 0
    assert c["QSEC"] % 256 == 0 and c["NVIS"] % 128 == 0 and DM % 512 == 0
    assert c["FC"] % 2 == 0
    return c


def build_program(cfg, stop_after=99, p2_nq=None, p2_kinds=("k", "v"), p2_flag=True, pre_only=False, debug_outs=False):
    S, DM, H, DFF = cfg["S"], cfg["DM"], cfg["H"], cfg["DFF"]
    OWN, NG = cfg["OWN"], cfg["NG"]
    HALF, DC, FC, MC, IG = cfg["HALF"], cfg["DC"], cfg["FC"], cfg["MC"], cfg["IG"]
    NBLK, NVIS, NVB, KEEP = cfg["NBLK"], cfg["NVIS"], cfg["NVB"], cfg["KEEP"]
    VIS_LO = cfg["VIS_LO"]
    M0, SCALE, EPS = cfg["M0"], cfg["SCALE"], cfg["EPS"]
    QSEC, NQ = cfg["QSEC"], cfg["NQ"]
    pair = cfg["pair_mode"]
    NVCOLB = (VIS_LO + NVIS + 127) // 128
    NTGO = OWN // 512

    nc = bass.Bass("TRN2", target_bir_lowering=False, num_devices=NC,
                   detect_race_conditions=False)

    def din(name, shape, dt=F32):
        return nc.dram_tensor(name, shape, dt, kind="ExternalInput").ap()

    x_own = din("x_own", [OWN, DM])
    x_kv = din("x_kv", [HALF, DM])
    meta = din("meta", [1, 16], I32)
    qwT = din("qwT", [DC, H, 128, 128])
    kwT = din("kwT", [DC, H, 128, 128])
    vwT = din("vwT", [DC, H, 128, 128])
    owT = din("owT", [DC, MC, 128, 128])
    gwT = din("gwT", [FC, DC, 128, 128], BF16)
    uwT = din("uwT", [FC, DC, 128, 128], BF16)
    dwT = din("dwT", [MC, FC, 128, 128], BF16)
    gs_in = din("gs_in", [128, FC])
    us_in = din("us_in", [128, FC])
    ds_in = din("ds_in", [128, MC])
    crope_own = din("crope_own", [128, OWN], BF16)
    srope_own = din("srope_own", [128, OWN], BF16)
    crope_kv = din("crope_kv", [128, HALF], BF16)
    srope_kv = din("srope_kv", [128, HALF], BF16)
    ct_kv = din("ct_kv", [HALF, 128])
    st_kv = din("st_kv", [HALF, 128])
    mstar = din("mstar", [128, 1008], BF16)
    dg1 = din("dg1", [128, DM], BF16)
    dg2a = din("dg2a", [128, DM], BF16)
    dg2b = din("dg2b", [128, DM], BF16)
    ltm = din("ltm", [128, NVB * NVIS], BF16)
    idnf_in = din("idnf_in", [128, 128])
    idnb_in = din("idnb_in", [128, 128], BF16)
    rot_in = din("rot_in", [128, 128], BF16)

    y_out = nc.dram_tensor("y_own", [OWN, DM], F32, kind="ExternalOutput").ap()
    if debug_outs:
        dbg_k = nc.dram_tensor("dbg_k", [2, 128, S], BF16, kind="ExternalOutput").ap()
        dbg_v = nc.dram_tensor("dbg_v", [2, 128, S // 128, 128], BF16, kind="ExternalOutput").ap()
        dbg_q = nc.dram_tensor("dbg_q", [2, 128, OWN], BF16, kind="ExternalOutput").ap()
        dbg_ffn = nc.dram_tensor("dbg_ffn", [OWN // 512, MC, 128, 512], BF16, kind="ExternalOutput").ap()
        dbg_imp = nc.dram_tensor("dbg_imp", [1, NVIS], F32, kind="ExternalOutput").ap()
        dbg_rank = nc.dram_tensor("dbg_rank", [1, NVIS], F32, kind="ExternalOutput").ap()
        dbg_qbar = nc.dram_tensor("dbg_qbar", [1, DM], F32, kind="ExternalOutput").ap()
        dbg_cs = nc.dram_tensor("dbg_cs", [4, 128, DM], F32, kind="ExternalOutput").ap()
        dbg_mask = nc.dram_tensor("dbg_mask", [128, NBLK], F32, kind="ExternalOutput").ap()
        dbg_attn = nc.dram_tensor("dbg_attn", [128, H * OWN], BF16, kind="ExternalOutput").ap()

    shk = dict(kind="Internal", addr_space="Shared") if pair else dict(kind="Internal")
    NHF = 2 if pair else 1
    kbufs = [nc.dram_tensor(f"kbuf{i}", [H, 128, HALF], BF16, **shk).ap()
             for i in range(NHF)]
    vbufs = [nc.dram_tensor(f"vbuf{i}", [H, 128, HALF // 128, 128], BF16, **shk).ap()
             for i in range(NHF)]
    csbuf = nc.dram_tensor("csbuf", [4, 128, DM], F32, **shk).ap()
    flagbuf = nc.dram_tensor("flagbuf", [2, 16], I32, **shk).ap()
    qbuf = nc.dram_tensor("qbuf", [H, 128, OWN], BF16, kind="Internal").ap()
    dbgbuf = nc.dram_tensor("dbgbuf", [128, 8192], BF16, kind="Internal").ap()
    ffnbuf = nc.dram_tensor("ffnbuf", [NTGO, MC, 128, 512], BF16, kind="Internal").ap()
    qbarbuf = nc.dram_tensor("qbarbuf", [1, DM], F32, kind="Internal").ap()
    impbuf = nc.dram_tensor("impbuf", [1, NVIS], F32, kind="Internal").ap()
    rankbuf = nc.dram_tensor("rankbuf", [1, NVIS], F32, kind="Internal").ap()

    flv = flagbuf.rearrange("a (x b) -> a x b", x=1)

    es = contextlib.ExitStack()

    def sbt(name, shape, dt):
        return es.enter_context(nc.sbuf_tensor(name, shape, dt))

    meta_sb = sbt("meta_sb", [1, 16], I32)
    idn_f = sbt("idn_f", [128, 128], F32)
    idn_b = sbt("idn_b", [128, 128], BF16)
    rot_b = sbt("rot_b", [128, 128], BF16)
    ones_b = sbt("ones_b", [128, 1], BF16)
    ones1f = sbt("ones1f", [1, 128], F32)
    gs_sb = sbt("gs_sb", [128, FC], F32)
    us_sb = sbt("us_sb", [128, FC], F32)
    ds_sb = sbt("ds_sb", [128, MC], F32)
    maskcol = sbt("maskcol", [128, NBLK], F32)
    small = sbt("small", [128, 16], F32)
    ep_sb = sbt("ep_sb", [1, 16], I32)
    epscol = sbt("epscol", [128, 1], F32)
    flag_sb = sbt("flag_sb", [1, 32], I32)

    pp = [es.enter_context(nc.psum_tensor(f"pp{i}", [128, 512], F32)) for i in range(6)]
    pbA = es.enter_context(nc.psum_tensor("pbA", [128, 1024], BF16))
    pbB = es.enter_context(nc.psum_tensor("pbB", [128, 1024], BF16))

    SEMS = {}

    def sem(name):
        if name not in SEMS:
            SEMS[name] = es.enter_context(nc.semaphore(name))
        return SEMS[name]

    block = es.enter_context(nc.Block())
    HOPS = {}

    def hop(e, key, op=None):
        if key not in HOPS:
            HOPS[key] = [sem("hop_" + key), 0]
        hh = HOPS[key]
        ops = op if isinstance(op, (list, tuple)) else [op]
        for o in ops:
            hh[1] += 1
            (o if o is not None else e.nop()).then_inc(hh[0], 1)
        e.wait_ge(hh[0], hh[1])
    njoin = [0]
    s_join = sem("join")
    ALLENG = (block.sync, block.gpsimd, block.tensor, block.scalar, block.vector)

    def join():
        njoin[0] += 1
        tgt = 5 * njoin[0]
        for engf in ALLENG:
            def _f(e, tgt=tgt):
                e.nop().then_inc(s_join, 1)
                e.wait_ge(s_join, tgt)
            engf(_f)


    def finish_early():
        s_fy = sem("fy")

        @block.sync
        def _(s):
            s.dma_start(out=y_out, in_=x_own).then_inc(s_fy, 16)
            s.wait_ge(s_fy, 16)
        join()
        es.close()
        return nc

    s_pre = sem("pre")

    @block.sync
    def _(s):
        for dst, src in ((meta_sb[0:1, :], meta), (idn_f[:, :], idnf_in),
                         (idn_b[:, :], idnb_in), (rot_b[:, :], rot_in),
                         (gs_sb[:, :], gs_in), (us_sb[:, :], us_in),
                         (ds_sb[:, :], ds_in)):
            s.dma_start(out=dst, in_=src).then_inc(sem("ld_pre"), 16)

    @block.vector
    def _(v):
        v.wait_ge(sem("ld_pre"), 16 * 7)
        v.memset(ones_b[:, 0:1], 1.0)
        v.memset(epscol[:, 0:1], float(EPS))
        v.memset(ones1f[0:1, :], 1.0).then_inc(s_pre, 1)

    for engf in (block.sync, block.gpsimd, block.tensor, block.scalar):
        def _f(e):
            e.wait_ge(s_pre, 1)
        engf(_f)
    if pre_only:
        return finish_early()

    SEM_BASE = {}

    def bsem(name):
        if name not in SEM_BASE:
            SEM_BASE[name] = 0
        return sem(name), SEM_BASE[name]

    def emit_ln(tag, src, row0, ntile, xtok, xln, xnT_dst, TSPAN, tcol0,
                with_cs=False, cstile=None, cs_stage=None, cs_first=False,
                cs_row0=0, csrc=None, ssrc=None):
        tag = "LN"
        s_x, bx = bsem(tag + "x")
        s_a, ba = bsem(tag + "a")
        s_b, bb = bsem(tag + "b")
        s_m, bm = bsem(tag + "m")
        s_e, be = bsem(tag + "e")
        s_c, bc = bsem(tag + "c")
        s_n, bn = bsem(tag + "n")
        s_v, bv = bsem(tag + "v")
        s_h, bh = bsem(tag + "h")
        nsec = ntile // 2
        pe_sec = (IG if with_cs else 0) + 2 * DC
        dve_sec = (2 * IG if with_cs else 0) + 2 * DC
        cs_off_m = IG if with_cs else 0
        cs_off_e = 2 * IG if with_cs else 0
        NXR = len(xln)

        @block.sync
        def _(s):
            for i in range(ntile):
                if i >= NXR:
                    s.wait_ge(s_n, bn + i - (NXR - 1))
                s.dma_start(out=xln[i % NXR][:, :],
                            in_=src[row0 + i * 128: row0 + (i + 1) * 128, :]
                            ).then_inc(s_x, 16)

        if with_cs:
            @block.gpsimd
            def _(g):
                for i in range(ntile):
                    if i >= 4:
                        g.wait_ge(s_m, bm + (i // 2 - 1) * pe_sec)
                    r0 = cs_row0 + i * 128
                    g.dma_start(out=cstile[2 * (i % 4)][:, :],
                                in_=csrc[r0:r0 + 128, :]).then_inc(s_c, 16)
                    g.dma_start(out=cstile[2 * (i % 4) + 1][:, :],
                                in_=ssrc[r0:r0 + 128, :]).then_inc(s_c, 16)


        @block.scalar
        def _(a):
            for i in range(ntile):
                xt = xln[i % NXR]
                dst = xtok[((i // 2) % 2) * 2 + (i % 2)]
                P = 8 * (i % 2)
                a.wait_ge(s_x, bx + 16 * (i + 1))
                if i >= 4:
                    a.wait_ge(s_m, bm + (i // 2 - 1) * pe_sec)
                a.activation(dst[:, :], xt[:, :], AF.Square,
                             accum_out=small[:, P + 2:P + 3]).then_inc(s_b, 1)
                a.wait_ge(s_v, bv + i + 1)
                a.activation(small[:, P + 5:P + 6], small[:, P + 4:P + 5], AF.Sqrt,
                             bias=epscol[:, 0:1]).then_inc(s_b, 1)
                if i % 2 == 1:
                    a.wait_ge(s_h, bh + i + 1)
                    a.activation(dst[:, :], xt[:, :], AF.Identity,
                                 scale=small[:, P + 6:P + 7],
                                 bias=small[:, P + 7:P + 8]).then_inc(s_n, 1)

        @block.tensor
        def _(t):
            for sct in range(nsec):
                t0 = xtok[(sct % 2) * 2 + 0]
                t1 = xtok[(sct % 2) * 2 + 1]
                t.wait_ge(s_n, bn + 2 * sct + 2)
                if with_cs:
                    t.wait_ge(s_c, bc + 32 * (2 * sct + 2))
                    ctA = cstile[2 * ((2 * sct) % 4)]
                    stA = cstile[2 * ((2 * sct) % 4) + 1]
                    ctB = cstile[2 * ((2 * sct) % 4) + 2]
                    stB = cstile[2 * ((2 * sct) % 4) + 3]
                    for ig in range(IG):
                        e = sct * IG + ig
                        csA = pp[0 + 2 * (e % 2)]
                        csB = pp[1 + 2 * (e % 2)]
                        if e >= 2:
                            ep_ = e - 2
                            t.wait_ge(s_e, be + (ep_ // IG) * dve_sec + ((ep_ % IG) + 1) * 2)
                        t.matmul(csA[:, :], ctA[:, :], t0[:, ig * 512:(ig + 1) * 512],
                                 start=True, stop=False)
                        t.matmul(csA[:, :], ctB[:, :], t1[:, ig * 512:(ig + 1) * 512],
                                 start=False, stop=True)
                        t.matmul(csB[:, :], stA[:, :], t0[:, ig * 512:(ig + 1) * 512],
                                 start=True, stop=False)
                        t.matmul(csB[:, :], stB[:, :], t1[:, ig * 512:(ig + 1) * 512],
                                 start=False, stop=True).then_inc(s_m, 1)
                for ti, tt_ in ((0, t0), (1, t1)):
                    for dci in range(DC):
                        k = sct * 2 * DC + ti * DC + dci
                        pbx = pbA if k % 2 == 0 else pbB
                        if k >= 2:
                            kp = k - 2
                            t.wait_ge(s_e, be + (kp // (2 * DC)) * dve_sec + cs_off_e
                                      + (kp % (2 * DC)) + 1)
                        t.transpose(pbx[:, 0:128], tt_[:, dci * 128:(dci + 1) * 128],
                                    idn_b[:, :]).then_inc(s_m, 1)

        @block.vector
        def _(v):
            def phase1(i):
                v.wait_ge(s_x, bx + 16 * (i + 1))
                P = 8 * (i % 2)
                op = v.reduce_sum(small[:, P:P + 1], xln[i % NXR][:, :], axis=AX)
                hop(v, "v", op)
                v.tensor_scalar(small[:, P + 1:P + 2], small[:, P:P + 1], 1.0 / DM,
                                None, ALU.mult).then_inc(s_a, 1)

            def phase2(j):
                P = 8 * (j % 2)
                v.wait_ge(s_b, bb + 2 * j + 1)
                v.wait_ge(s_a, ba + j + 1)
                op = v.tensor_tensor(small[:, P + 3:P + 4], small[:, P + 1:P + 2],
                                     small[:, P + 1:P + 2], ALU.mult)
                hop(v, "v", op)
                v.scalar_tensor_tensor(small[:, P + 4:P + 5], small[:, P + 2:P + 3],
                                       1.0 / DM, small[:, P + 3:P + 4],
                                       ALU.mult, ALU.subtract).then_inc(s_v, 1)
                v.wait_ge(s_b, bb + 2 * j + 2)
                op = v.reciprocal(small[:, P + 6:P + 7], small[:, P + 5:P + 6])
                hop(v, "v", op)
                v.scalar_tensor_tensor(small[:, P + 7:P + 8], small[:, P + 1:P + 2],
                                       -1.0, small[:, P + 6:P + 7],
                                       ALU.mult, ALU.mult).then_inc(s_h, 1)
                if j % 2 == 0:
                    v.wait_ge(s_h, bh + j + 1)
                    dst = xtok[((j // 2) % 2) * 2 + (j % 2)]
                    v.tensor_scalar(dst[:, :], xln[j % NXR][:, :],
                                    small[:, P + 6:P + 7], small[:, P + 7:P + 8],
                                    ALU.mult, ALU.add).then_inc(s_n, 1)

            def sec_work(sct):
                base_m = sct * pe_sec
                if with_cs:
                    for ig in range(IG):
                        e = sct * IG + ig
                        csA = pp[0 + 2 * (e % 2)]
                        csB = pp[1 + 2 * (e % 2)]
                        v.wait_ge(s_m, bm + base_m + ig + 1)
                        first = cs_first and sct == 0
                        if first:
                            v.tensor_copy(cs_stage[0][:, ig * 512:(ig + 1) * 512], csA[:, :])
                            v.tensor_copy(cs_stage[1][:, ig * 512:(ig + 1) * 512], csB[:, :]
                                          ).then_inc(s_e, 2)
                        else:
                            sl0 = cs_stage[0][:, ig * 512:(ig + 1) * 512]
                            sl1 = cs_stage[1][:, ig * 512:(ig + 1) * 512]
                            v.tensor_add(sl0, csA[:, :], sl0)
                            v.tensor_add(sl1, csB[:, :], sl1).then_inc(s_e, 2)
                for ti in range(2):
                    for dci in range(DC):
                        k = sct * 2 * DC + ti * DC + dci
                        pbx = pbA if k % 2 == 0 else pbB
                        v.wait_ge(s_m, bm + base_m + cs_off_m + ti * DC + dci + 1)
                        tcol = tcol0 + (sct * 2 + ti) * 128
                        v.tensor_copy(xnT_dst[:, dci * TSPAN + tcol: dci * TSPAN + tcol + 128],
                                      pbx[:, 0:128]).then_inc(s_e, 1)

            for i in range(ntile + 1):
                if i < ntile:
                    phase1(i)
                if i >= 1:
                    phase2(i - 1)
                if i >= 2 and i % 2 == 0:
                    sec_work(i // 2 - 1)

        SEM_BASE["LNx"] = bx + 16 * ntile
        SEM_BASE["LNa"] = ba + ntile
        SEM_BASE["LNv"] = bv + ntile
        SEM_BASE["LNh"] = bh + ntile
        SEM_BASE["LNb"] = bb + 2 * ntile
        SEM_BASE["LNm"] = bm + nsec * pe_sec
        SEM_BASE["LNe"] = be + nsec * dve_sec
        SEM_BASE["LNc"] = bc + 32 * ntile
        SEM_BASE["LNn"] = bn + ntile

    PJN = [0]

    def emit_proj(tag, kind, wsrc, xnT_src, TSPAN, ntg, crop, srop, wch, stag,
                  vtmp, ksb, store_fn, nheads):
        tag = "PJ"
        PJN[0] += 1
        pjid = PJN[0]
        s_w, bw = bsem(tag + "w")
        s_p, bp = bsem(tag + "p")
        s_r, br = bsem(tag + "r")
        s_cp, bcp = bsem(tag + "cp")
        s_ev, bev = bsem(tag + "ev")
        s_st, bst = bsem(tag + "st")
        pertg_ev = 1 if kind != "v" else 4

        @block.gpsimd
        def _(g):
            if pair:
                hr = g.alloc_register(f"pjhr{pjid}")
                g.reg_load(hr, meta_sb[0:1, 1:2])
                g._half_reg = hr

            NSP = 4

            def ldw(h):
                if h >= 2:
                    g.wait_ge(s_p, bp + (h - 1) * ntg)
                dsp = DC // NSP
                for sp in range(NSP):
                    g.dma_start(out=wch[h % 2][:, sp * dsp * 128:(sp + 1) * dsp * 128]
                                .rearrange("p (dc f) -> p dc f", f=128),
                                in_=wsrc[sp * dsp:(sp + 1) * dsp, h, :, :]
                                .rearrange("dc p f -> p dc f")).then_inc(s_w, 16)
            ldw(0)
            if nheads > 1:
                ldw(1)
            for h in range(nheads):
                g.wait_ge(s_ev, bev + (h + 1) * ntg * pertg_ev)
                if DBG["store"] == "local":
                    g.dma_start(out=dbgbuf[:, 0:stag.shape[1] if hasattr(stag, "shape") else 1024],
                                in_=stag[:, :]).then_inc(s_st, 16)
                elif DBG["store"]:
                    store_fn(g, h)
                if h + 2 < nheads:
                    ldw(h + 2)

        @block.tensor
        def _(t):
            ntot = nheads * ntg

            def post(ep):
                t.wait_ge(s_cp, bcp + ep + 1)
                if kind != "v":
                    if ep >= 1:
                        t.wait_ge(s_ev, bev + ep)
                    t.matmul(pp[4][:, :], rot_b[:, :], ksb[ep % 2][:, :],
                             start=True, stop=True).then_inc(s_r, 1)
                else:
                    for ci in range(4):
                        k = ep * 4 + ci
                        pbx = pbA if k % 2 == 0 else pbB
                        if k >= 2:
                            t.wait_ge(s_ev, bev + k - 1)
                        t.transpose(pbx[:, 0:128], vtmp[ep % 2][:, ci * 128:(ci + 1) * 128],
                                    idn_b[:, :]).then_inc(s_r, 1)

            for h in range(nheads):
                t.wait_ge(s_w, bw + 64 * (h + 1))
                for tg in range(ntg):
                    e = h * ntg + tg
                    psA = pp[2 + (e % 2)]
                    if e >= 2:
                        t.wait_ge(s_cp, bcp + e - 1)
                    last = None
                    for dci in range(DC):
                        last = t.matmul(psA[:, :], wch[h % 2][:, dci * 128:(dci + 1) * 128],
                                        xnT_src[:, dci * TSPAN + tg * 512: dci * TSPAN + tg * 512 + 512],
                                        start=(dci == 0), stop=(dci == DC - 1))
                    last.then_inc(s_p, 1)
                    if e >= 1:
                        post(e - 1)
            post(ntot - 1)

        @block.vector
        def _(v):
            for h in range(nheads):
                for tg in range(ntg):
                    e = h * ntg + tg
                    psA = pp[2 + (e % 2)]
                    v.wait_ge(s_p, bp + e + 1)
                    if h > 0 and tg == 0:
                        v.wait_ge(s_st, bst + 16 * h)
                    if kind == "v":
                        if e >= 2:
                            v.wait_ge(s_r, br + 4 * (e - 1))
                        v.tensor_copy(vtmp[e % 2][:, :], psA[:, :]).then_inc(s_cp, 1)
                        for ci in range(4):
                            k = e * 4 + ci
                            pbx = pbA if k % 2 == 0 else pbB
                            v.wait_ge(s_r, br + k + 1)
                            v.tensor_copy(stag[:, tg * 512 + ci * 128: tg * 512 + ci * 128 + 128],
                                          pbx[:, 0:128]).then_inc(s_ev, 1)
                    else:
                        if e >= 2:
                            v.wait_ge(s_r, br + e - 1)
                        v.tensor_copy(ksb[e % 2][:, :], psA[:, :]).then_inc(s_cp, 1)
                        v.wait_ge(s_cp, bcp + e + 1)
                        v.wait_ge(s_r, br + e + 1)
                        cs_ = crop[:, tg * 512: tg * 512 + 512]
                        ss_ = srop[:, tg * 512: tg * 512 + 512]
                        if DBG["rope"]:
                            op1 = v.tensor_tensor(ksb[2][:, :], ksb[e % 2][:, :], cs_, ALU.mult)
                            op2 = v.tensor_tensor(ksb[3][:, :], pp[4][:, :], ss_, ALU.mult)
                            hop(v, "v", [op1, op2])
                            v.tensor_add(stag[:, tg * 512: tg * 512 + 512],
                                         ksb[2][:, :], ksb[3][:, :]).then_inc(s_ev, 1)
                        else:
                            v.tensor_copy(stag[:, tg * 512: tg * 512 + 512],
                                          ksb[e % 2][:, :]).then_inc(s_ev, 1)


        SEM_BASE["PJw"] = bw + 64 * nheads
        SEM_BASE["PJp"] = bp + nheads * ntg
        SEM_BASE["PJr"] = br + nheads * ntg * (1 if kind != "v" else 4)
        SEM_BASE["PJcp"] = bcp + nheads * ntg
        SEM_BASE["PJev"] = bev + nheads * ntg * pertg_ev
        SEM_BASE["PJst"] = bst + 16 * nheads

    p2 = contextlib.ExitStack()

    def sb2(name, shape, dt):
        return p2.enter_context(nc.sbuf_tensor(name, shape, dt))

    xln2 = [sb2(f"xln2_{i}", [128, DM], F32) for i in range(2)]
    xtok2 = [sb2(f"xtok2_{i}", [128, DM], BF16) for i in range(4)]
    cstile2 = [sb2(f"cst2_{i}", [128, 128], BF16) for i in range(8)]
    cs_stage = [sb2(f"css_{i}", [128, DM], F32) for i in range(2)]
    xnT2 = sb2("xnT2", [128, DC * QSEC], BF16)
    wch2 = [sb2(f"wch2_{i}", [128, DC * 128], BF16) for i in range(2)]
    kst2 = sb2("kst2", [128, QSEC], BF16)
    vst2 = sb2("vst2", [128, QSEC], BF16)
    vtmp2 = [sb2(f"vtmp2_{i}", [128, 512], BF16) for i in range(2)]
    ksb2 = [sb2(f"ksb2_{i}", [128, 512], BF16) for i in range(4)]
    crkv2 = sb2("crkv2", [128, QSEC], BF16)
    srkv2 = sb2("srkv2", [128, QSEC], BF16)

    s_p2rope = sem("p2rope")
    s_p2cs = sem("p2cs")
    NTGQ = QSEC // 512

    NQ_EMIT = NQ if p2_nq is None else p2_nq
    for q in range(NQ_EMIT):
        @block.sync
        def _(s, q=q):
            s.dma_start(out=crkv2[:, :],
                        in_=crope_kv[:, q * QSEC:(q + 1) * QSEC]).then_inc(s_p2rope, 16)
            s.dma_start(out=srkv2[:, :],
                        in_=srope_kv[:, q * QSEC:(q + 1) * QSEC]).then_inc(s_p2rope, 16)
            s.wait_ge(s_p2rope, 32 * (q + 1))

        emit_ln(f"L{q}", x_kv, q * QSEC, QSEC // 128, xtok2, xln2, xnT2,
                QSEC, 0, with_cs=True, cstile=cstile2, cs_stage=cs_stage,
                cs_first=(q == 0), cs_row0=q * QSEC, csrc=ct_kv, ssrc=st_kv)
        join()

        if q == NQ_EMIT - 1:
            @block.gpsimd
            def _(g):
                if pair:
                    hrc = g.alloc_register("cshr")
                    g.reg_load(hrc, meta_sb[0:1, 1:2])
                for ci in range(2):
                    if pair:
                        with g.If_cmp(hrc, 0, "IS_EQ"):
                            g.dma_start(out=csbuf[ci, :, :],
                                        in_=cs_stage[ci][:, :]).then_inc(s_p2cs, 16)
                        with g.Else():
                            g.dma_start(out=csbuf[2 + ci, :, :],
                                        in_=cs_stage[ci][:, :]).then_inc(s_p2cs, 16)
                    else:
                        g.dma_start(out=csbuf[ci, :, :],
                                    in_=cs_stage[ci][:, :]).then_inc(s_p2cs, 16)

        for kindw, wsrc_ in [(kk, {"k": kwT, "v": vwT}[kk]) for kk in p2_kinds]:
            tagq = f"P2{kindw}{q}"
            s_stq = sem("PJst")

            if kindw == "k":
                def store_fn(g, h, q=q, s_stq=s_stq):
                    def wr(hf, h=h, q=q, s_stq=s_stq):
                        g.dma_start(out=kbufs[hf][h, :, q * QSEC:(q + 1) * QSEC],
                                    in_=kst2[:, :]).then_inc(s_stq, 16)
                    if pair:
                        with g.If_cmp(g._half_reg, 0, "IS_EQ"):
                            wr(0)
                        with g.Else():
                            wr(1)
                    else:
                        wr(0)
            else:
                def store_fn(g, h, q=q, s_stq=s_stq):
                    tb0 = q * (QSEC // 128)
                    def wr(hf, h=h, tb0=tb0, s_stq=s_stq):
                        g.dma_start(out=vbufs[hf][h, :, tb0:tb0 + QSEC // 128, :],
                                    in_=vst2[:, :].rearrange("p (tb d) -> p tb d", d=128)
                                    ).then_inc(s_stq, 16)
                    if pair:
                        with g.If_cmp(g._half_reg, 0, "IS_EQ"):
                            wr(0)
                        with g.Else():
                            wr(1)
                    else:
                        wr(0)

            emit_proj(tagq, kindw, wsrc_, xnT2, QSEC, NTGQ, crkv2, srkv2,
                      wch2, kst2 if kindw == "k" else vst2,
                      vtmp2, ksb2, store_fn, H)
            join()

    @block.gpsimd
    def _(g):
        if p2_flag and p2_kinds:
            g.wait_ge(sem("PJst"), SEM_BASE["PJst"])
        if p2_flag:
            g.wait_ge(s_p2cs, 32)
        if pair and p2_flag:
            g.memset(ep_sb[0:1, :], 0).then_inc(sem("flg0"), 1)
            g.wait_ge(sem("flg0"), 1)
            g.dma_start(out=ep_sb[0:1, 0:1], in_=meta[0:1, 0:1]).then_inc(sem("flg1"), 16)
            g.wait_ge(sem("flg1"), 16)
            hrf = g.alloc_register("flghr")
            g.reg_load(hrf, meta_sb[0:1, 1:2])
            with g.If_cmp(hrf, 0, "IS_EQ"):
                g.dma_start(out=flv[0, :, :], in_=ep_sb[0:1, :]).then_inc(sem("flg"), 16)
            with g.Else():
                g.dma_start(out=flv[1, :, :], in_=ep_sb[0:1, :]).then_inc(sem("flg"), 16)
            g.wait_ge(sem("flg"), 16)

    p2.close()
    join()
    if stop_after <= 2:
        return finish_early()

    pxn = contextlib.ExitStack()
    xnTo = pxn.enter_context(nc.sbuf_tensor("xnTo", [128, DC * OWN], BF16))
    p01 = contextlib.ExitStack()

    def sb01(name, shape, dt):
        return p01.enter_context(nc.sbuf_tensor(name, shape, dt))

    xln1 = [sb01(f"xln1_{i}", [128, DM], F32) for i in range(3)]
    xtok1 = [sb01(f"xtok1_{i}", [128, DM], BF16) for i in range(4)]
    wch1 = [sb01(f"wch1_{i}", [128, DC * 128], BF16) for i in range(2)]
    qst1 = sb01("qst1", [128, OWN], BF16)
    ksb1 = [sb01(f"ksb1_{i}", [128, 512], BF16) for i in range(4)]
    cro1 = sb01("cro1", [128, OWN], BF16)
    sro1 = sb01("sro1", [128, OWN], BF16)

    @block.sync
    def _(s):
        s.dma_start(out=cro1[:, :], in_=crope_own).then_inc(sem("p1rope"), 16)
        s.dma_start(out=sro1[:, :], in_=srope_own).then_inc(sem("p1rope"), 16)
        s.wait_ge(sem("p1rope"), 32)

    emit_ln("LO", x_own, 0, OWN // 128, xtok1, xln1, xnTo, OWN, 0, with_cs=False)
    join()

    def store_q(g, h):
        g.dma_start(out=qbuf[h, :, :], in_=qst1[:, :]).then_inc(sem("PJst"), 16)

    emit_proj("P1q", "q", qwT, xnTo, OWN, NTGO, cro1, sro1, wch1,
              qst1, None, ksb1, store_q, H)
    join()
    p01.close()
    if stop_after <= 3:
        return finish_early()

    p3 = contextlib.ExitStack()

    def sb3(name, shape, dt):
        return p3.enter_context(nc.sbuf_tensor(name, shape, dt))

    gch = [sb3(f"gch_{i}", [128, DC * 64], BF16) for i in range(2)]
    uch = [sb3(f"uch_{i}", [128, DC * 64], BF16) for i in range(2)]
    dch = [sb3(f"dch_{i}", [128, FC * 64], BF16) for i in range(2)]
    h_sb = sb3("h_sb", [128, FC * 512], BF16)
    gate_t = [sb3(f"gate_t{i}", [128, 512], BF16) for i in range(2)]
    ffn_t = [sb3(f"ffn_t{i}", [128, 512], BF16) for i in range(2)]

    s3gl = sem("p3gl")
    s3gm = sem("p3gm")
    s3si = sem("p3si")
    s3hc = sem("p3hc")
    s3dl = sem("p3dl")
    s3dm = sem("p3dm")
    s3de = sem("p3de")
    s3fs = sem("p3fs")

    for tg in range(NTGO):
        b_gl, b_gm, b_si, b_hc = 64 * FC * tg, 4 * FC * tg, FC * tg, FC * tg
        b_dl, b_dm, b_de, b_fs = 32 * MC * tg, 2 * MC * tg, MC * tg, 16 * MC * tg

        @block.sync
        def _(s, tg=tg, b_gm=b_gm, b_dm=b_dm, b_de=b_de, b_fs=b_fs):
            for fc in range(FC):
                for hf in range(2):
                    d0 = hf * (DC // 2)
                    if fc >= 1:
                        s.wait_ge(s3gm, b_gm + 4 * (fc - 1) + hf + 1)
                    s.dma_start(out=gch[hf][:, :].rearrange("p (dc f) -> p dc f", f=128),
                                in_=gwT[fc, d0:d0 + DC // 2, :, :].rearrange("dc p f -> p dc f")
                                ).then_inc(s3gl, 16)
                for hf in range(2):
                    d0 = hf * (DC // 2)
                    if fc >= 1:
                        s.wait_ge(s3gm, b_gm + 4 * (fc - 1) + hf + 3)
                    s.dma_start(out=uch[hf][:, :].rearrange("p (dc f) -> p dc f", f=128),
                                in_=uwT[fc, d0:d0 + DC // 2, :, :].rearrange("dc p f -> p dc f")
                                ).then_inc(s3gl, 16)
            for mc in range(MC):
                for hf in range(2):
                    gh = 2 * mc + hf
                    if gh >= 2:
                        s.wait_ge(s3dm, b_dm + gh - 1)
                    f0 = hf * (FC // 2)
                    s.dma_start(out=dch[gh % 2][:, :].rearrange("p (fc f) -> p fc f", f=128),
                                in_=dwT[mc, f0:f0 + FC // 2, :, :].rearrange("fc p f -> p fc f")
                                ).then_inc(s3dl, 16)
                if mc >= 2:
                    s.wait_ge(s3de, b_de + mc - 1)
                    s.dma_start(out=ffnbuf[tg, mc - 2, :, :], in_=ffn_t[(mc - 2) % 2][:, :]
                                ).then_inc(s3fs, 16)
            for mc in range(MC - 2, MC):
                s.wait_ge(s3de, b_de + mc + 1)
                s.dma_start(out=ffnbuf[tg, mc, :, :], in_=ffn_t[mc % 2][:, :]
                            ).then_inc(s3fs, 16)

        @block.tensor
        def _(t, tg=tg, b_gl=b_gl, b_si=b_si, b_hc=b_hc, b_dl=b_dl, b_de=b_de):
            for fc in range(FC):
                psg = pp[2 + (fc % 2)]
                psu = pp[4 + (fc % 2)]
                if fc >= 2:
                    t.wait_ge(s3si, b_si + fc - 1)
                    t.wait_ge(s3hc, b_hc + fc - 1)
                for hf in range(2):
                    t.wait_ge(s3gl, b_gl + 64 * fc + 16 * (hf + 1))
                    last = None
                    for di in range(DC // 2):
                        dci = hf * (DC // 2) + di
                        last = t.matmul(psg[:, :], gch[hf][:, di * 128:(di + 1) * 128],
                                        xnTo[:, dci * OWN + tg * 512: dci * OWN + tg * 512 + 512],
                                        start=(dci == 0), stop=(dci == DC - 1))
                    last.then_inc(s3gm, 1)
                for hf in range(2):
                    t.wait_ge(s3gl, b_gl + 64 * fc + 16 * (hf + 3))
                    last = None
                    for di in range(DC // 2):
                        dci = hf * (DC // 2) + di
                        last = t.matmul(psu[:, :], uch[hf][:, di * 128:(di + 1) * 128],
                                        xnTo[:, dci * OWN + tg * 512: dci * OWN + tg * 512 + 512],
                                        start=(dci == 0), stop=(dci == DC - 1))
                    last.then_inc(s3gm, 1)
            for mc in range(MC):
                t.wait_ge(s3hc, b_hc + FC)
                psd = pp[2 + (mc % 2)]
                if mc >= 2:
                    t.wait_ge(s3de, b_de + mc - 1)
                for hf in range(2):
                    gh = 2 * mc + hf
                    t.wait_ge(s3dl, b_dl + 16 * (gh + 1))
                    last = None
                    for fi in range(FC // 2):
                        fc = hf * (FC // 2) + fi
                        last = t.matmul(psd[:, :], dch[gh % 2][:, fi * 128:(fi + 1) * 128],
                                        h_sb[:, fc * 512:(fc + 1) * 512],
                                        start=(fc == 0), stop=(fc == FC - 1))
                    last.then_inc(s3dm, 1)

        @block.scalar
        def _(a, tg=tg, b_gm=b_gm, b_dm=b_dm, b_hc=b_hc, b_fs=b_fs):
            for fc in range(FC):
                a.wait_ge(s3gm, b_gm + 4 * fc + 2)
                if fc >= 2:
                    a.wait_ge(s3hc, b_hc + fc - 1)
                psg = pp[2 + (fc % 2)]
                a.activation(gate_t[fc % 2][:, :], psg[:, :], AF.Silu,
                             scale=gs_sb[:, fc:fc + 1]).then_inc(s3si, 1)
            for mc in range(MC):
                a.wait_ge(s3dm, b_dm + 2 * (mc + 1))
                if mc >= 2:
                    a.wait_ge(s3fs, b_fs + 16 * (mc - 1))
                psd = pp[2 + (mc % 2)]
                a.activation(ffn_t[mc % 2][:, :], psd[:, :], AF.Copy,
                             scale=ds_sb[:, mc:mc + 1]).then_inc(s3de, 1)

        @block.vector
        def _(v, tg=tg, b_gm=b_gm, b_si=b_si):
            for fc in range(FC):
                v.wait_ge(s3gm, b_gm + 4 * fc + 4)
                v.wait_ge(s3si, b_si + fc + 1)
                psu = pp[4 + (fc % 2)]
                v.scalar_tensor_tensor(h_sb[:, fc * 512:(fc + 1) * 512], psu[:, :],
                                       us_sb[:, fc:fc + 1], gate_t[fc % 2][:, :],
                                       ALU.mult, ALU.mult).then_inc(s3hc, 1)


    @block.sync
    def _(s):
        s.wait_ge(s3fs, 16 * MC * NTGO)
    join()
    p3.close()
    pxn.close()
    if stop_after <= 4:
        return finish_early()

    p4 = contextlib.ExitStack()

    def sb4(name, shape, dt):
        return p4.enter_context(nc.sbuf_tensor(name, shape, dt))

    cs_a = sb4("cs_a", [128, DM], F32)
    cs_bq = sb4("cs_bq", [128, DM], F32)
    cs_d = sb4("cs_d", [128, DM], F32)
    C_sb = sb4("C_sb", [128, DM], BF16)
    S_sb = sb4("S_sb", [128, DM], BF16)
    wct = sb4("wct", [128, DM + 128], F32)
    wst = sb4("wst", [128, DM + 128], F32)
    wq4 = sb4("wq4", [128, DC * 512], BF16)
    dgm = [sb4(f"dgm{i}", [128, DM], BF16) for i in range(3)]
    comb4 = sb4("comb4", [128, DM], BF16)
    qrow = sb4("qrow", [1, DM], F32)
    qb_sb = sb4("qb_sb", [128, H], BF16)
    kvis4 = [sb4(f"kvis4_{i}", [128, NVIS], BF16) for i in range(2)]
    imp_row = sb4("imp_row", [1, NVIS], F32)
    imp_part = sb4("imp_part", [128, NVB], F32)
    imp_bc = sb4("imp_bc", [128, NVIS], F32)
    gtb = sb4("gtb", [128, NVIS], BF16)
    eqb = sb4("eqb", [128, NVIS], BF16)
    ltm_sb = sb4("ltm_sb", [128, NVB * NVIS], BF16)
    rrow = sb4("rrow", [1, NVIS], F32)
    mrank = sb4("mrank", [128, NVCOLB], F32)

    s4 = sem("p4chain")
    k4 = [0]
    d4 = {}

    def step(engf, fn):
        tgt = k4[0]

        def _f(e, tgt=tgt, fn=fn):
            if tgt > 0:
                e.wait_ge(s4, tgt)
            r = fn(e)
            if r is None:
                e.nop().then_inc(s4, 1)
            else:
                r.then_inc(s4, 1)
        engf(_f)
        k4[0] += 1

    def dstep(engf, fn):
        tgt = k4[0]
        key = engf.__func__.__name__ if hasattr(engf, "__func__") else engf.__name__
        if key not in d4:
            d4[key] = [sem(f"p4dma_{key}"), 0]
        sd = d4[key]
        sd[1] += 16
        dtgt = sd[1]

        def _f(e, tgt=tgt, fn=fn, dtgt=dtgt, sd=sd):
            if tgt > 0:
                e.wait_ge(s4, tgt)
            fn(e).then_inc(sd[0], 16)
            e.wait_ge(sd[0], dtgt)
            e.nop().then_inc(s4, 1)
        engf(_f)
        k4[0] += 1

    if pair:
        s4poll = sem("p4poll")

        def poll(g):
            epv = g.value_load(meta_sb[0:1, 0:1])
            cnt = g.alloc_register("p4cnt")
            f0 = g.alloc_register("p4f0")
            f1 = g.alloc_register("p4f1")
            g.reg_mov(cnt, 0)
            g.reg_mov(f0, 0)
            g.reg_mov(f1, 0)
            with g.While(lambda: (g.compute_val(f0) != epv) | (g.compute_val(f1) != epv)):
                g.reg_add(cnt, cnt, 16)
                g.dma_start(out=flag_sb[0:1, 0:32],
                            in_=flagbuf.rearrange("a b -> (a b)")[None, :]
                            ).then_inc(s4poll, 16)
                g.wait_ge(s4poll, cnt)
                g.reg_load(f0, flag_sb[0:1, 0:1])
                g.reg_load(f1, flag_sb[0:1, 16:17])
            return None
        step(block.gpsimd, poll)

    dstep(block.sync, lambda s: s.dma_start(out=cs_a[:, :], in_=csbuf[0, :, :]))
    if pair:
        dstep(block.sync, lambda s: s.dma_start(out=cs_bq[:, :], in_=csbuf[2, :, :]))
        step(block.vector, lambda v: v.tensor_add(cs_a[:, :], cs_a[:, :], cs_bq[:, :]))
    dstep(block.sync, lambda s: s.dma_start(out=cs_bq[:, :], in_=csbuf[1, :, :]))
    if pair:
        dstep(block.sync, lambda s: s.dma_start(out=cs_d[:, :], in_=csbuf[3, :, :]))
        step(block.vector, lambda v: v.tensor_add(cs_bq[:, :], cs_bq[:, :], cs_d[:, :]))

    for (srcm, dstm) in ((cs_a, C_sb), (cs_bq, S_sb)):
        for ic in range(DC):
            def tr2(t, srcm=srcm, ic=ic):
                return t.transpose(pp[5][:, 0:128], srcm[:, ic * 128:(ic + 1) * 128],
                                   idn_f[:, :])
            step(block.tensor, tr2)

            def ev(v, dstm=dstm, ic=ic):
                return v.tensor_copy(dstm[:, ic * 128:(ic + 1) * 128], pp[5][:, 0:128])
            step(block.vector, ev)

    dstep(block.sync, lambda s: s.dma_start(out=dgm[0][:, :], in_=dg1))
    dstep(block.sync, lambda s: s.dma_start(out=dgm[1][:, :], in_=dg2a))
    dstep(block.sync, lambda s: s.dma_start(out=dgm[2][:, :], in_=dg2b))
    step(block.vector, lambda v: v.memset(wst[:, 0:64], 0.0))
    step(block.vector, lambda v: v.memset(wst[:, 64 + DM:128 + DM], 0.0))

    for fg in range(IG):
        for hh in range(4):
            dstep(block.gpsimd,
                  lambda g, fg=fg, hh=hh: g.dma_start(
                      out=wq4[:, :].rearrange("p (dc g) -> p dc g", g=512)[:, :, hh * 128:(hh + 1) * 128],
                      in_=qwT[:, 4 * fg + hh, :, :].rearrange("dc p f -> p dc f")))

        def mms(t, fg=fg):
            last = None
            for ic in range(DC):
                t.matmul(pp[2][:, :], C_sb[:, ic * 128:(ic + 1) * 128],
                         wq4[:, ic * 512:(ic + 1) * 512],
                         start=(ic == 0), stop=(ic == DC - 1))
            for ic in range(DC):
                last = t.matmul(pp[3][:, :], S_sb[:, ic * 128:(ic + 1) * 128],
                                wq4[:, ic * 512:(ic + 1) * 512],
                                start=(ic == 0), stop=(ic == DC - 1))
            return last
        step(block.tensor, mms)

        def evw(v, fg=fg):
            v.tensor_copy(wct[:, 64 + fg * 512: 64 + (fg + 1) * 512], pp[2][:, :])
            return v.tensor_copy(wst[:, 64 + fg * 512: 64 + (fg + 1) * 512], pp[3][:, :])
        step(block.vector, evw)

    def qcomb(v):
        op1 = v.tensor_tensor(cs_a[:, :], dgm[0][:, :], wct[:, 64:64 + DM], ALU.mult)
        op2 = v.tensor_tensor(cs_bq[:, :], dgm[1][:, :], wst[:, 128:128 + DM], ALU.mult)
        hop(v, "v", [op1, op2])
        op3 = v.tensor_add(cs_a[:, :], cs_a[:, :], cs_bq[:, :])
        op4 = v.tensor_tensor(cs_bq[:, :], dgm[2][:, :], wst[:, 0:DM], ALU.mult)
        hop(v, "v", [op3, op4])
        return v.tensor_add(comb4[:, :], cs_a[:, :], cs_bq[:, :])
    step(block.vector, qcomb)

    for fg in range(IG):
        step(block.tensor,
             lambda t, fg=fg: t.matmul(pp[2][0:1, 0:512], ones_b[:, 0:1],
                                       comb4[:, fg * 512:(fg + 1) * 512],
                                       start=True, stop=True))
        step(block.vector,
             lambda v, fg=fg: v.tensor_copy(qrow[0:1, fg * 512:(fg + 1) * 512],
                                            pp[2][0:1, 0:512]))

    dstep(block.sync, lambda s: s.dma_start(out=qbarbuf, in_=qrow[0:1, :]))
    dstep(block.gpsimd, lambda g: g.dma_start(
        out=qb_sb[:, :], in_=qbarbuf.rearrange("a (h d) -> d (a h)", d=128),
        allow_slow_non_contiguous=True))

    for h in range(H):
        dstep(block.sync,
              lambda s, h=h: s.dma_start(out=kvis4[h % 2][:, :],
                                         in_=kbufs[0][h, :, VIS_LO:VIS_LO + NVIS]))
        step(block.tensor,
             lambda t, h=h: t.matmul(pp[3][0:1, 0:NVIS], qb_sb[:, h:h + 1],
                                     kvis4[h % 2][:, :],
                                     start=(h == 0), stop=(h == H - 1)))
    step(block.vector, lambda v: v.tensor_copy(imp_row[0:1, :], pp[3][0:1, 0:NVIS]))
    dstep(block.sync, lambda s: s.dma_start(out=impbuf, in_=imp_row[0:1, :]))
    dstep(block.sync, lambda s: s.dma_start(
        out=imp_part[:, :], in_=impbuf.rearrange("a (b p) -> p (a b)", p=128),
        allow_slow_non_contiguous=True))
    dstep(block.sync, lambda s: s.dma_start(out=ltm_sb[:, :], in_=ltm))

    step(block.tensor,
         lambda t: t.matmul(pp[4][:, 0:NVIS], ones1f[0:1, :], imp_row[0:1, :],
                            start=True, stop=True))
    step(block.vector, lambda v: v.tensor_copy(imp_bc[:, :], pp[4][:, 0:NVIS]))

    for ic2 in range(NVB):
        def rnk(v, ic2=ic2):
            op1 = v.tensor_scalar(gtb[:, :], imp_bc[:, :], imp_part[:, ic2:ic2 + 1], None, ALU.is_lt)
            op2 = v.tensor_scalar(eqb[:, :], imp_bc[:, :], imp_part[:, ic2:ic2 + 1], None, ALU.is_equal)
            hop(v, "v", [op1, op2])
            op3 = v.tensor_tensor(eqb[:, :], eqb[:, :], ltm_sb[:, ic2 * NVIS:(ic2 + 1) * NVIS], ALU.mult)
            hop(v, "v", op3)
            return v.tensor_add(gtb[:, :], gtb[:, :], eqb[:, :])
        step(block.vector, rnk)
        step(block.tensor,
             lambda t, ic2=ic2: t.matmul(pp[5][0:1, 0:NVIS], ones_b[:, 0:1], gtb[:, :],
                                         start=(ic2 == 0), stop=(ic2 == NVB - 1)))
    step(block.vector, lambda v: v.tensor_copy(rrow[0:1, :], pp[5][0:1, 0:NVIS]))
    dstep(block.sync, lambda s: s.dma_start(out=rankbuf, in_=rrow[0:1, :]))

    step(block.vector, lambda v: v.memset(mrank[:, :], 0.0))
    rb = rankbuf.rearrange("a j -> (a j)")
    p0_ = VIS_LO
    while p0_ < VIS_LO + NVIS:
        b = p0_ // 128
        pend = min((b + 1) * 128, VIS_LO + NVIS)

        def ldr(s, pa=p0_, pb_=pend, b=b):
            return s.dma_start(out=mrank[pa - b * 128: pb_ - b * 128, b:b + 1],
                               in_=rb[pa - VIS_LO:pb_ - VIS_LO].rearrange("(j x) -> j x", x=1))
        dstep(block.sync, ldr)
        p0_ = pend

    def mkmask(v):
        op0 = v.memset(maskcol[:, :], -float(M0))
        op1 = v.tensor_scalar(mrank[:, :], mrank[:, :], float(KEEP) - 0.5, None, ALU.is_ge)
        hop(v, "v", [op0, op1])
        return v.tensor_scalar(maskcol[:, 0:NVCOLB], mrank[:, :], -1e30, -float(M0),
                               ALU.mult, ALU.add)
    step(block.vector, mkmask)
    join()
    p4.close()
    if stop_after <= 5:
        return finish_early()

    attn_all = es.enter_context(nc.sbuf_tensor("attn_all", [128, H * OWN], BF16))
    p5 = contextlib.ExitStack()

    def sb5(name, shape, dt):
        return p5.enter_context(nc.sbuf_tensor(name, shape, dt))

    mst_sb = sb5("mst_sb", [128, 1008], BF16)
    kh5 = [sb5(f"kh5_{i}", [128, S], BF16) for i in range(2)]
    vh5 = [sb5(f"vh5_{i}", [128, S], BF16) for i in range(2)]
    qh5 = [sb5(f"qh5_{i}", [128, OWN], BF16) for i in range(2)]
    pt5 = [sb5(f"pt5_{i}", [128, 512], BF16) for i in range(4)]
    lrow = sb5("lrow", [1, 512], F32)
    rlbc = sb5("rlbc", [128, 512], F32)
    LOOK = 3

    s5ld = sem("p5ld")
    s5sm = sem("p5sm")
    s5ex = sem("p5ex")
    s5mk = sem("p5mk")
    s5pv = sem("p5pv")
    s5fin = sem("p5fin")
    s5fpe = sem("p5fpe")

    sched = []
    for g_ in range(NG):
        for j in range(32 * g_ + 32):
            sched.append((g_, j, 32 * g_ <= j, j == 32 * g_ + 31))
    PPH = len(sched)
    diag_pfx = [0]
    fin_pfx = [0]
    for (_, _, dg_, la_) in sched:
        diag_pfx.append(diag_pfx[-1] + (1 if dg_ else 0))
        fin_pfx.append(fin_pfx[-1] + (1 if la_ else 0))
    DPH = diag_pfx[-1]

    LDH = 16 * (2 * NHF + 1)

    @block.sync
    def _(s):
        s.dma_start(out=mst_sb[:, :], in_=mstar).then_inc(s5ld, 16)
        for h in range(H):
            if h >= 2:
                s.wait_ge(s5pv, (h - 1) * PPH)
            for hf in range(NHF):
                s.dma_start(out=kh5[h % 2][:, hf * HALF:(hf + 1) * HALF],
                            in_=kbufs[hf][h, :, :]).then_inc(s5ld, 16)
                s.dma_start(out=vh5[h % 2][:, hf * HALF:(hf + 1) * HALF],
                            in_=vbufs[hf][h, :, :, :].rearrange("p tb d -> p (tb d)")
                            ).then_inc(s5ld, 16)
            s.dma_start(out=qh5[h % 2][:, :], in_=qbuf[h, :, :]).then_inc(s5ld, 16)

    @block.tensor
    def _(t):
        for h in range(H):
            def s_pass(pi, h=h):
                g_, j, dg_, la_ = sched[pi]
                e = h * PPH + pi
                if e >= 4:
                    t.wait_ge(s5ex, e - 3)
                t.matmul(pp[e % 4][:, :], kh5[h % 2][:, j * 128:(j + 1) * 128],
                         qh5[h % 2][:, g_ * 512:(g_ + 1) * 512],
                         start=True, stop=True).then_inc(s5sm, 1)

            def pv_pass(pi, h=h):
                g_, j, dg_, la_ = sched[pi]
                e = h * PPH + pi
                fe = h * NG + g_
                first = (j == 0)
                if first:
                    t.wait_ge(s5fin, 3 * fe)
                if dg_:
                    t.wait_ge(s5mk, h * DPH + diag_pfx[pi + 1])
                else:
                    t.wait_ge(s5ex, e + 1)
                t.matmul(pp[4][:, :], vh5[h % 2][:, j * 128:(j + 1) * 128],
                         pt5[e % 4][:, :], start=first, stop=la_)
                t.matmul(pp[5][0:1, 0:512], ones_b[:, 0:1], pt5[e % 4][:, :],
                         start=first, stop=la_).then_inc(s5pv, 1)
                if la_:
                    t.wait_ge(s5fin, 3 * fe + 2)
                    t.matmul(pp[5][:, :], ones1f[0:1, :], lrow[0:1, :],
                             start=True, stop=True).then_inc(s5fpe, 1)

            t.wait_ge(s5ld, 16 + LDH * (h + 1))
            for pi in range(PPH + LOOK):
                if pi < PPH:
                    s_pass(pi)
                if pi >= LOOK:
                    pv_pass(pi - LOOK)

    @block.scalar
    def _(a):
        for h in range(H):
            for pi, (g_, j, dg_, la_) in enumerate(sched):
                e = h * PPH + pi
                a.wait_ge(s5sm, e + 1)
                if e >= 4:
                    a.wait_ge(s5pv, e - 3)
                a.activation(pt5[e % 4][:, :], pp[e % 4][:, :], AF.Exp,
                             scale=float(SCALE), bias=maskcol[:, j:j + 1]
                             ).then_inc(s5ex, 1)

    @block.vector
    def _(v):
        v.wait_ge(s5ld, 16)
        for h in range(H):
            for pi, (g_, j, dg_, la_) in enumerate(sched):
                e = h * PPH + pi
                if dg_:
                    v.wait_ge(s5ex, e + 1)
                    r = j - 32 * g_
                    u0 = 16 * (31 - r)
                    v.tensor_tensor(pt5[e % 4][:, :], pt5[e % 4][:, :],
                                    mst_sb[:, u0:u0 + 512], ALU.mult
                                    ).then_inc(s5mk, 1)
                if la_:
                    fe = h * NG + g_
                    v.wait_ge(s5pv, e + 1)
                    v.tensor_copy(lrow[0:1, :], pp[5][0:1, 0:512]).then_inc(s5fin, 1)
                    v.wait_ge(s5fin, 3 * fe + 1)
                    v.reciprocal(lrow[0:1, :], lrow[0:1, :]).then_inc(s5fin, 1)
                    v.wait_ge(s5fpe, fe + 1)
                    op = v.tensor_copy(rlbc[:, :], pp[5][:, :])
                    hop(v, "v", op)
                    v.tensor_tensor(attn_all[:, h * OWN + g_ * 512: h * OWN + (g_ + 1) * 512],
                                    pp[4][:, :], rlbc[:, :], ALU.mult
                                    ).then_inc(s5fin, 1)

    join()
    p5.close()
    if stop_after <= 6:
        return finish_early()

    p6 = contextlib.ExitStack()

    def sb6(name, shape, dt):
        return p6.enter_context(nc.sbuf_tensor(name, shape, dt))

    och = [sb6(f"och_{i}", [128, DC * 128], BF16) for i in range(2)]
    ffn6 = [sb6(f"ffn6_{i}", [128, 512], BF16) for i in range(2)]
    sum6 = [sb6(f"sum6_{i}", [128, 512], F32) for i in range(2)]
    xch6 = [sb6(f"xch6_{i}", [128, 512], F32) for i in range(2)]
    yt6 = [sb6(f"yt6_{i}", [128, 512], F32) for i in range(2)]

    s6ol = sem("p6ol")
    s6fl = sem("p6fl")
    s6xm = sem("p6xm")
    s6sm = sem("p6sm")
    s6tr = sem("p6tr")
    s6xl = sem("p6xl")
    s6ya = sem("p6ya")
    s6ys = sem("p6ys")
    NE6 = MC * NTGO

    @block.gpsimd
    def _(g):
        for mc in range(MC):
            if mc >= 2:
                g.wait_ge(s6xm, (mc - 1) * NTGO)
            g.dma_start(out=och[mc % 2][:, :].rearrange("p (dc f) -> p dc f", f=128),
                        in_=owT[:, mc, :, :].rearrange("dc p f -> p dc f")
                        ).then_inc(s6ol, 16)

    def ypo(e):
        mc2, tg2 = e // NTGO, e % NTGO
        return tg2 * 512, mc2 * 128

    @block.sync
    def _(s):
        for mc in range(MC):
            for tg in range(NTGO):
                e = mc * NTGO + tg
                if e >= 2:
                    s.wait_ge(s6sm, e - 1)
                s.dma_start(out=ffn6[e % 2][:, :], in_=ffnbuf[tg, mc, :, :]
                            ).then_inc(s6fl, 16)
                if e >= 1:
                    s.wait_ge(s6ya, 4 * e)
                    r0p, c0p = ypo(e - 1)
                    s.dma_start(out=y_out[r0p:r0p + 512, c0p:c0p + 128]
                                .rearrange("(ci p) d -> p ci d", p=128),
                                in_=yt6[(e - 1) % 2][:, :]
                                .rearrange("p (ci d) -> p ci d", d=128)
                                ).then_inc(s6ys, 16)
                r0 = tg * 512
                s.dma_start(out=xch6[e % 2][:, :].rearrange("p (ci d) -> p ci d", d=128),
                            in_=x_own[r0:r0 + 512, mc * 128:(mc + 1) * 128]
                            .rearrange("(ci p) d -> p ci d", p=128)
                            ).then_inc(s6xl, 16)
        s.wait_ge(s6ya, 4 * NE6)
        r0p, c0p = ypo(NE6 - 1)
        s.dma_start(out=y_out[r0p:r0p + 512, c0p:c0p + 128]
                    .rearrange("(ci p) d -> p ci d", p=128),
                    in_=yt6[(NE6 - 1) % 2][:, :]
                    .rearrange("p (ci d) -> p ci d", d=128)).then_inc(s6ys, 16)
        s.wait_ge(s6ys, 16 * NE6)

    @block.tensor
    def _(t):
        for mc in range(MC):
            t.wait_ge(s6ol, 16 * (mc + 1))
            for tg in range(NTGO):
                e = mc * NTGO + tg
                ps_om = pp[2 + (e % 2)]
                if e >= 2:
                    t.wait_ge(s6sm, e - 1)
                last = None
                for fc in range(DC):
                    last = t.matmul(ps_om[:, :], och[mc % 2][:, fc * 128:(fc + 1) * 128],
                                    attn_all[:, fc * OWN + tg * 512: fc * OWN + tg * 512 + 512],
                                    start=(fc == 0), stop=(fc == DC - 1))
                last.then_inc(s6xm, 1)
                t.wait_ge(s6sm, e + 1)
                for ci in range(4):
                    k = 4 * e + ci
                    pbx = pp[4 + (k % 2)]
                    if k >= 2:
                        t.wait_ge(s6ya, k - 1)
                    t.transpose(pbx[:, 0:128], sum6[e % 2][:, ci * 128:(ci + 1) * 128],
                                idn_f[:, :]).then_inc(s6tr, 1)

    @block.vector
    def _(v):
        for mc in range(MC):
            for tg in range(NTGO):
                e = mc * NTGO + tg
                ps_om = pp[2 + (e % 2)]
                v.wait_ge(s6xm, e + 1)
                v.wait_ge(s6fl, 16 * (e + 1))
                if e >= 2:
                    v.wait_ge(s6tr, 4 * (e - 1))
                v.tensor_add(sum6[e % 2][:, :], ps_om[:, :], ffn6[e % 2][:, :]
                             ).then_inc(s6sm, 1)
                v.wait_ge(s6xl, 16 * (e + 1))
                if e >= 2:
                    v.wait_ge(s6ys, 16 * (e - 1))
                for ci in range(4):
                    k = 4 * e + ci
                    pbx = pp[4 + (k % 2)]
                    v.wait_ge(s6tr, k + 1)
                    v.tensor_add(yt6[e % 2][:, ci * 128:(ci + 1) * 128], pbx[:, 0:128],
                                 xch6[e % 2][:, ci * 128:(ci + 1) * 128]
                                 ).then_inc(s6ya, 1)


    if debug_outs:
        s_dbg = sem("dbg")

        @block.sync
        def _(s):
            nd = 0
            for hh in range(2):
                if NHF == 2:
                    for hf in range(2):
                        s.dma_start(out=dbg_k[hh, :, hf * HALF:(hf + 1) * HALF],
                                    in_=kbufs[hf][hh, :, :]).then_inc(s_dbg, 16); nd += 1
                        s.dma_start(out=dbg_v[hh, :, hf * (HALF // 128):(hf + 1) * (HALF // 128), :],
                                    in_=vbufs[hf][hh, :, :, :]).then_inc(s_dbg, 16); nd += 1
                else:
                    s.dma_start(out=dbg_k[hh, :, :], in_=kbufs[0][hh, :, :]).then_inc(s_dbg, 16); nd += 1
                    s.dma_start(out=dbg_v[hh, :, :, :], in_=vbufs[0][hh, :, :, :]).then_inc(s_dbg, 16); nd += 1
                s.dma_start(out=dbg_q[hh, :, :], in_=qbuf[hh, :, :]).then_inc(s_dbg, 16); nd += 1
            s.dma_start(out=dbg_ffn, in_=ffnbuf).then_inc(s_dbg, 16); nd += 1
            s.dma_start(out=dbg_imp, in_=impbuf).then_inc(s_dbg, 16); nd += 1
            s.dma_start(out=dbg_rank, in_=rankbuf).then_inc(s_dbg, 16); nd += 1
            s.dma_start(out=dbg_qbar, in_=qbarbuf).then_inc(s_dbg, 16); nd += 1
            s.dma_start(out=dbg_cs, in_=csbuf).then_inc(s_dbg, 16); nd += 1
            s.dma_start(out=dbg_mask, in_=maskcol[:, :]).then_inc(s_dbg, 16); nd += 1
            s.dma_start(out=dbg_attn, in_=attn_all[:, :]).then_inc(s_dbg, 16); nd += 1
            s.wait_ge(s_dbg, 16 * nd)

    join()
    p6.close()
    es.close()
    return nc



def host_prepare(cfg, inputs, epoch):
    S, DM, H, DFF = cfg["S"], cfg["DM"], cfg["H"], cfg["DFF"]
    OWN, HALF = cfg["OWN"], cfg["HALF"]
    DC, FC, MC, NVB, NVIS = cfg["DC"], cfg["FC"], cfg["MC"], cfg["NVB"], cfg["NVIS"]
    pair = cfg["pair_mode"]

    x = np.ascontiguousarray(np.asarray(inputs["L0_attn_hidden_states"])[0])
    qw = np.asarray(inputs["L0_attn_q_w"])
    kw = np.asarray(inputs["L0_attn_k_w"])
    vw = np.asarray(inputs["L0_attn_v_w"])
    ow = np.asarray(inputs["L0_attn_o_w"])
    gw = np.asarray(inputs["L0_ffn_gate_w"])
    uw = np.asarray(inputs["L0_ffn_up_w"])
    dw = np.asarray(inputs["L0_ffn_down_w"])
    gs = np.asarray(inputs["L0_ffn_gate_s"]).astype(np.float32)
    us = np.asarray(inputs["L0_ffn_up_s"]).astype(np.float32)
    ds = np.asarray(inputs["L0_ffn_down_s"]).astype(np.float32)

    def tileT(w):
        o, i = w.shape
        return np.ascontiguousarray(
            w.T.reshape(i // 128, 128, o // 128, 128).transpose(0, 2, 1, 3))

    qwT = tileT(qw).astype(np.float32)
    kwT = tileT(kw).astype(np.float32)
    vwT = tileT(vw).astype(np.float32)
    owT = tileT(ow).astype(np.float32)
    gwT = np.ascontiguousarray(tileT(gw.astype(np.float32)).transpose(1, 0, 2, 3)).astype(BF)
    uwT = np.ascontiguousarray(tileT(uw.astype(np.float32)).transpose(1, 0, 2, 3)).astype(BF)
    dwT_ = np.ascontiguousarray(tileT(dw.astype(np.float32)).transpose(1, 0, 2, 3)).astype(BF)

    gs_t = np.ascontiguousarray(gs.reshape(FC, 128).T)
    us_t = np.ascontiguousarray(us.reshape(FC, 128).T)
    ds_t = np.ascontiguousarray(ds.reshape(MC, 128).T)

    inv_freq = 1.0 / 10000.0 ** (np.arange(0, 128, 2) / 128.0)

    def rope_tables(pos):
        ang = pos[None, :] * inv_freq[:, None]
        cr = np.concatenate([np.cos(ang), np.cos(ang)], 0).astype(BF)
        sr = np.concatenate([np.sin(ang), np.sin(ang)], 0).astype(BF)
        return np.ascontiguousarray(cr), np.ascontiguousarray(sr)

    dd = np.arange(128)[:, None]
    fm = np.arange(DM)[None, :] % 128
    dg1 = (dd == fm).astype(np.float32).astype(BF)
    dg2a = (-((fm < 64) & (dd == fm)).astype(np.float32)).astype(BF)
    dg2b = ((fm >= 64) & (dd == fm)).astype(np.float32).astype(BF)

    ltm = np.zeros((128, NVB * NVIS), BF)
    for ic in range(NVB):
        ii = ic * 128 + np.arange(128)[:, None]
        jj = np.arange(NVIS)[None, :]
        ltm[:, ic * NVIS:(ic + 1) * NVIS] = (ii < jj).astype(BF)

    idnf = np.eye(128, dtype=np.float32)
    idnb = np.eye(128, dtype=BF)
    rot = np.zeros((128, 128), np.float32)
    for d in range(64):
        rot[d + 64, d] = -1.0
    for d in range(64, 128):
        rot[d - 64, d] = 1.0
    rot = rot.astype(BF)

    iff = np.concatenate([inv_freq, inv_freq])

    in_maps = []
    for c in range(NC):
        half = c % 2 if pair else 0
        t0 = half * HALF if pair else 0
        pos_own = (np.arange(OWN) * 8 + c).astype(np.float64)
        cro, sro = rope_tables(pos_own)
        crk, srk = rope_tables(np.arange(t0, t0 + HALF).astype(np.float64))
        pos_kv = np.arange(t0, t0 + HALF)
        ang = pos_kv[:, None] * iff[None, :]
        msk = ((pos_kv >= cfg["TEXT_LO"]) & (pos_kv < cfg["TEXT_HI"]))[:, None]
        ctk = np.ascontiguousarray((np.cos(ang) * msk).astype(np.float32))
        stk = np.ascontiguousarray((np.sin(ang) * msk).astype(np.float32))
        kp = np.arange(128)[:, None]
        uu = np.arange(1008)[None, :]
        mstar = np.ascontiguousarray((kp <= 8 * uu - 3968 + c).astype(BF))
        mt = np.zeros((1, 16), np.int32)
        mt[0, 0] = epoch
        mt[0, 1] = half
        mt[0, 2] = c
        in_maps.append({
            "x_own": np.ascontiguousarray(x[c::8, :]),
            "x_kv": np.ascontiguousarray(x[t0:t0 + HALF, :]),
            "meta": mt,
            "qwT": qwT, "kwT": kwT, "vwT": vwT, "owT": owT,
            "gwT": gwT, "uwT": uwT, "dwT": dwT_,
            "gs_in": gs_t, "us_in": us_t, "ds_in": ds_t,
            "crope_own": cro, "srope_own": sro,
            "crope_kv": crk, "srope_kv": srk,
            "ct_kv": ctk, "st_kv": stk,
            "mstar": mstar,
            "dg1": dg1, "dg2a": dg2a, "dg2b": dg2b,
            "ltm": np.ascontiguousarray(ltm),
            "idnf_in": idnf, "idnb_in": idnb, "rot_in": np.ascontiguousarray(rot),
        })
    return in_maps


_PROG_CACHE = {}


def kernel(**inputs):
    cfg = make_cfg()
    if "full" not in _PROG_CACHE:
        _PROG_CACHE["full"] = build_program(cfg)
    nc = _PROG_CACHE["full"]
    import random
    epoch = random.randint(1, 2 ** 30)
    in_maps = host_prepare(cfg, inputs, epoch)
    res = run_bass_kernel_spmd(nc, in_maps, core_ids=list(range(NC)))
    S, DM = cfg["S"], cfg["DM"]
    y = np.empty((1, S, DM), np.float32)
    for c in range(NC):
        y[0, c::8, :] = res.results[c]["y_own"]
    return y


# version 10
# speedup vs baseline: 1.1305x; 1.1305x over previous
import math
import contextlib
import numpy as np
import ml_dtypes
import concourse.bass as bass
import concourse.mybir as mybir
from concourse.bass_utils import run_bass_kernel_spmd

F32 = mybir.dt.float32
BF16 = mybir.dt.bfloat16
I32 = mybir.dt.int32
AF = mybir.ActivationFunctionType
ALU = mybir.AluOpType
AX = mybir.AxisListType.X
BF = ml_dtypes.bfloat16

NC = 8
DBG = {"store": True, "rot": True, "rope": True, "mm": True, "cp": True, "rmm": True}


def make_cfg(S=8192, DM=4096, DFF=11008, VIS_LO=1, VIS_HI=513, KEEP=256,
             EXP_ACTIONS=56, pair_mode=True, qsec=1024):
    c = dict(S=S, DM=DM, DFF=DFF, VIS_LO=VIS_LO, VIS_HI=VIS_HI, KEEP=KEEP)
    c["H"] = DM // 128
    c["OWN"] = S // NC
    c["NG"] = c["OWN"] // 512
    c["HALF"] = S // 2 if pair_mode else S
    c["DC"] = DM // 128
    c["FC"] = DFF // 128
    c["MC"] = DM // 128
    c["IG"] = DM // 512
    c["NBLK"] = S // 128
    c["NVIS"] = VIS_HI - VIS_LO
    c["NVB"] = c["NVIS"] // 128
    c["TEXT_LO"] = VIS_HI
    c["TEXT_HI"] = max(VIS_HI, S - EXP_ACTIONS)
    c["M0"] = 40.0
    c["SCALE"] = 1.0 / math.sqrt(128.0)
    c["EPS"] = 1e-5
    c["pair_mode"] = pair_mode
    c["QSEC"] = min(qsec, c["HALF"])
    c["NQ"] = c["HALF"] // c["QSEC"]
    assert c["OWN"] % 512 == 0 and c["HALF"] % c["QSEC"] == 0
    assert c["QSEC"] % 256 == 0 and c["NVIS"] % 128 == 0 and DM % 512 == 0
    assert c["FC"] % 2 == 0
    return c


def build_program(cfg, stop_after=99, p2_nq=None, p2_kinds=("k", "v"), p2_flag=True, pre_only=False, debug_outs=False):
    S, DM, H, DFF = cfg["S"], cfg["DM"], cfg["H"], cfg["DFF"]
    OWN, NG = cfg["OWN"], cfg["NG"]
    HALF, DC, FC, MC, IG = cfg["HALF"], cfg["DC"], cfg["FC"], cfg["MC"], cfg["IG"]
    NBLK, NVIS, NVB, KEEP = cfg["NBLK"], cfg["NVIS"], cfg["NVB"], cfg["KEEP"]
    VIS_LO = cfg["VIS_LO"]
    M0, SCALE, EPS = cfg["M0"], cfg["SCALE"], cfg["EPS"]
    QSEC, NQ = cfg["QSEC"], cfg["NQ"]
    pair = cfg["pair_mode"]
    NVCOLB = (VIS_LO + NVIS + 127) // 128
    NTGO = OWN // 512

    nc = bass.Bass("TRN2", target_bir_lowering=False, num_devices=NC,
                   detect_race_conditions=False)

    def din(name, shape, dt=F32):
        return nc.dram_tensor(name, shape, dt, kind="ExternalInput").ap()

    x_own = din("x_own", [OWN, DM])
    x_kv = din("x_kv", [HALF, DM])
    meta = din("meta", [1, 16], I32)
    qwT = din("qwT", [DC, H, 128, 128])
    kwT = din("kwT", [DC, H, 128, 128])
    vwT = din("vwT", [DC, H, 128, 128])
    owT = din("owT", [DC, MC, 128, 128])
    gwT = din("gwT", [FC, DC, 128, 128], BF16)
    uwT = din("uwT", [FC, DC, 128, 128], BF16)
    dwT = din("dwT", [MC, FC, 128, 128], BF16)
    gs_in = din("gs_in", [128, FC])
    us_in = din("us_in", [128, FC])
    ds_in = din("ds_in", [128, MC])
    crope_own = din("crope_own", [128, OWN], BF16)
    srope_own = din("srope_own", [128, OWN], BF16)
    crope_kv = din("crope_kv", [128, HALF], BF16)
    srope_kv = din("srope_kv", [128, HALF], BF16)
    ct_kv = din("ct_kv", [HALF, 128])
    st_kv = din("st_kv", [HALF, 128])
    mstar = din("mstar", [128, 1008], BF16)
    dg1 = din("dg1", [128, DM], BF16)
    dg2a = din("dg2a", [128, DM], BF16)
    dg2b = din("dg2b", [128, DM], BF16)
    ltm = din("ltm", [128, NVB * NVIS], BF16)
    idnf_in = din("idnf_in", [128, 128])
    idnb_in = din("idnb_in", [128, 128], BF16)
    rot_in = din("rot_in", [128, 128], BF16)

    y_out = nc.dram_tensor("y_own", [OWN, DM], F32, kind="ExternalOutput").ap()
    if debug_outs:
        dbg_k = nc.dram_tensor("dbg_k", [2, 128, S], BF16, kind="ExternalOutput").ap()
        dbg_v = nc.dram_tensor("dbg_v", [2, 128, S // 128, 128], BF16, kind="ExternalOutput").ap()
        dbg_q = nc.dram_tensor("dbg_q", [2, 128, OWN], BF16, kind="ExternalOutput").ap()
        dbg_ffn = nc.dram_tensor("dbg_ffn", [OWN // 512, MC, 128, 512], BF16, kind="ExternalOutput").ap()
        dbg_imp = nc.dram_tensor("dbg_imp", [1, NVIS], F32, kind="ExternalOutput").ap()
        dbg_rank = nc.dram_tensor("dbg_rank", [1, NVIS], F32, kind="ExternalOutput").ap()
        dbg_qbar = nc.dram_tensor("dbg_qbar", [1, DM], F32, kind="ExternalOutput").ap()
        dbg_cs = nc.dram_tensor("dbg_cs", [4, 128, DM], F32, kind="ExternalOutput").ap()
        dbg_mask = nc.dram_tensor("dbg_mask", [128, NBLK], F32, kind="ExternalOutput").ap()
        dbg_attn = nc.dram_tensor("dbg_attn", [128, H * OWN], BF16, kind="ExternalOutput").ap()

    shk = dict(kind="Internal", addr_space="Shared") if pair else dict(kind="Internal")
    NHF = 2 if pair else 1
    kbufs = [nc.dram_tensor(f"kbuf{i}", [H, 128, HALF], BF16, **shk).ap()
             for i in range(NHF)]
    vbufs = [nc.dram_tensor(f"vbuf{i}", [H, 128, HALF // 128, 128], BF16, **shk).ap()
             for i in range(NHF)]
    csbuf = nc.dram_tensor("csbuf", [4, 128, DM], F32, **shk).ap()
    flagbuf = nc.dram_tensor("flagbuf", [2, 16], I32, **shk).ap()
    qbuf = nc.dram_tensor("qbuf", [H, 128, OWN], BF16, kind="Internal").ap()
    dbgbuf = nc.dram_tensor("dbgbuf", [128, 8192], BF16, kind="Internal").ap()
    ffnbuf = nc.dram_tensor("ffnbuf", [NTGO, MC, 128, 512], BF16, kind="Internal").ap()
    qbarbuf = nc.dram_tensor("qbarbuf", [1, DM], F32, kind="Internal").ap()
    impbuf = nc.dram_tensor("impbuf", [1, NVIS], F32, kind="Internal").ap()
    rankbuf = nc.dram_tensor("rankbuf", [1, NVIS], F32, kind="Internal").ap()

    flv = flagbuf.rearrange("a (x b) -> a x b", x=1)

    es = contextlib.ExitStack()

    def sbt(name, shape, dt):
        return es.enter_context(nc.sbuf_tensor(name, shape, dt))

    meta_sb = sbt("meta_sb", [1, 16], I32)
    idn_f = sbt("idn_f", [128, 128], F32)
    idn_b = sbt("idn_b", [128, 128], BF16)
    rot_b = sbt("rot_b", [128, 128], BF16)
    ones_b = sbt("ones_b", [128, 1], BF16)
    ones1f = sbt("ones1f", [1, 128], F32)
    gs_sb = sbt("gs_sb", [128, FC], F32)
    us_sb = sbt("us_sb", [128, FC], F32)
    ds_sb = sbt("ds_sb", [128, MC], F32)
    maskcol = sbt("maskcol", [128, NBLK], F32)
    small = sbt("small", [128, 16], F32)
    ep_sb = sbt("ep_sb", [1, 16], I32)
    epscol = sbt("epscol", [128, 1], F32)
    flag_sb = sbt("flag_sb", [1, 32], I32)

    pp = [es.enter_context(nc.psum_tensor(f"pp{i}", [128, 512], F32)) for i in range(6)]
    pbA = es.enter_context(nc.psum_tensor("pbA", [128, 1024], BF16))
    pbB = es.enter_context(nc.psum_tensor("pbB", [128, 1024], BF16))

    SEMS = {}

    def sem(name):
        if name not in SEMS:
            SEMS[name] = es.enter_context(nc.semaphore(name))
        return SEMS[name]

    block = es.enter_context(nc.Block())
    HOPS = {}

    def hop(e, key, op=None):
        if key not in HOPS:
            HOPS[key] = [sem("hop_" + key), 0]
        hh = HOPS[key]
        ops = op if isinstance(op, (list, tuple)) else [op]
        for o in ops:
            hh[1] += 1
            (o if o is not None else e.nop()).then_inc(hh[0], 1)
        e.wait_ge(hh[0], hh[1])
    njoin = [0]
    s_join = sem("join")
    ALLENG = (block.sync, block.gpsimd, block.tensor, block.scalar, block.vector)

    def join():
        njoin[0] += 1
        tgt = 5 * njoin[0]
        for engf in ALLENG:
            def _f(e, tgt=tgt):
                e.nop().then_inc(s_join, 1)
                e.wait_ge(s_join, tgt)
            engf(_f)


    def finish_early():
        s_fy = sem("fy")

        @block.sync
        def _(s):
            s.dma_start(out=y_out, in_=x_own).then_inc(s_fy, 16)
            s.wait_ge(s_fy, 16)
        join()
        es.close()
        return nc

    s_pre = sem("pre")

    @block.sync
    def _(s):
        for dst, src in ((meta_sb[0:1, :], meta), (idn_f[:, :], idnf_in),
                         (idn_b[:, :], idnb_in), (rot_b[:, :], rot_in),
                         (gs_sb[:, :], gs_in), (us_sb[:, :], us_in),
                         (ds_sb[:, :], ds_in)):
            s.dma_start(out=dst, in_=src).then_inc(sem("ld_pre"), 16)

    @block.vector
    def _(v):
        v.wait_ge(sem("ld_pre"), 16 * 7)
        v.memset(ones_b[:, 0:1], 1.0)
        v.memset(epscol[:, 0:1], float(EPS))
        v.memset(ones1f[0:1, :], 1.0).then_inc(s_pre, 1)

    for engf in (block.sync, block.gpsimd, block.tensor, block.scalar):
        def _f(e):
            e.wait_ge(s_pre, 1)
        engf(_f)
    if pre_only:
        return finish_early()

    SEM_BASE = {}

    def bsem(name):
        if name not in SEM_BASE:
            SEM_BASE[name] = 0
        return sem(name), SEM_BASE[name]

    def emit_ln(tag, src, row0, ntile, xtok, xln, xnT_dst, TSPAN, tcol0,
                with_cs=False, cstile=None, cs_stage=None, cs_first=False,
                cs_row0=0, csrc=None, ssrc=None):
        tag = "LN"
        s_x, bx = bsem(tag + "x")
        s_a, ba = bsem(tag + "a")
        s_b, bb = bsem(tag + "b")
        s_m, bm = bsem(tag + "m")
        s_e, be = bsem(tag + "e")
        s_c, bc = bsem(tag + "c")
        s_n, bn = bsem(tag + "n")
        s_v, bv = bsem(tag + "v")
        s_h, bh = bsem(tag + "h")
        nsec = ntile // 2
        pe_sec = (IG if with_cs else 0) + 2 * DC
        dve_sec = 2 * IG if with_cs else 0
        TPG = (2 * DC) // IG if with_cs else 0
        s_e2, be2 = bsem(tag + "e2")
        NXR = len(xln)

        def msc(sct, ks):
            if with_cs:
                return bm + sct * pe_sec + (ks // TPG) * (1 + TPG) + 1 + (ks % TPG) + 1
            return bm + sct * pe_sec + ks + 1

        def msc_cs(sct, ig):
            return bm + sct * pe_sec + ig * (1 + TPG) + 1

        @block.sync
        def _(s):
            for i in range(ntile):
                if i >= NXR:
                    s.wait_ge(s_n, bn + i - (NXR - 1))
                s.dma_start(out=xln[i % NXR][:, :],
                            in_=src[row0 + i * 128: row0 + (i + 1) * 128, :]
                            ).then_inc(s_x, 16)

        if with_cs:
            @block.gpsimd
            def _(g):
                for i in range(ntile):
                    if i >= 4:
                        g.wait_ge(s_m, bm + (i // 2 - 1) * pe_sec)
                    r0 = cs_row0 + i * 128
                    g.dma_start(out=cstile[2 * (i % 4)][:, :],
                                in_=csrc[r0:r0 + 128, :]).then_inc(s_c, 16)
                    g.dma_start(out=cstile[2 * (i % 4) + 1][:, :],
                                in_=ssrc[r0:r0 + 128, :]).then_inc(s_c, 16)


        @block.scalar
        def _(a):
            for i in range(ntile):
                xt = xln[i % NXR]
                dst = xtok[((i // 2) % 2) * 2 + (i % 2)]
                P = 8 * (i % 2)
                a.wait_ge(s_x, bx + 16 * (i + 1))
                if i >= 4:
                    a.wait_ge(s_m, bm + (i // 2 - 1) * pe_sec)
                a.activation(dst[:, :], xt[:, :], AF.Square,
                             accum_out=small[:, P + 2:P + 3]).then_inc(s_b, 1)
                a.wait_ge(s_v, bv + i + 1)
                a.activation(small[:, P + 5:P + 6], small[:, P + 4:P + 5], AF.Sqrt,
                             bias=epscol[:, 0:1]).then_inc(s_b, 1)
                if i % 2 == 1:
                    sct = (i - 1) // 2
                    for ti in range(2):
                        for dci in range(DC):
                            k = sct * 2 * DC + ti * DC + dci
                            pbx = pbA if k % 2 == 0 else pbB
                            a.wait_ge(s_m, msc(sct, ti * DC + dci))
                            tcol = tcol0 + (sct * 2 + ti) * 128
                            a.activation(
                                xnT_dst[:, dci * TSPAN + tcol: dci * TSPAN + tcol + 128],
                                pbx[:, 0:128], AF.Copy, scale=1.0).then_inc(s_e2, 1)

        @block.tensor
        def _(t):
            for sct in range(nsec):
                t0 = xtok[(sct % 2) * 2 + 0]
                t1 = xtok[(sct % 2) * 2 + 1]
                t.wait_ge(s_n, bn + 2 * sct + 2)
                def tr_one(ks, t0=t0, t1=t1, sct=sct):
                    ti, dci = ks // DC, ks % DC
                    tt_ = t0 if ti == 0 else t1
                    k = sct * 2 * DC + ks
                    pbx = pbA if k % 2 == 0 else pbB
                    if k >= 2:
                        t.wait_ge(s_e2, be2 + k - 1)
                    t.transpose(pbx[:, 0:128], tt_[:, dci * 128:(dci + 1) * 128],
                                idn_b[:, :]).then_inc(s_m, 1)

                if with_cs:
                    t.wait_ge(s_c, bc + 32 * (2 * sct + 2))
                    ctA = cstile[2 * ((2 * sct) % 4)]
                    stA = cstile[2 * ((2 * sct) % 4) + 1]
                    ctB = cstile[2 * ((2 * sct) % 4) + 2]
                    stB = cstile[2 * ((2 * sct) % 4) + 3]
                    for ig in range(IG):
                        e = sct * IG + ig
                        csA = pp[0 + 2 * (e % 2)]
                        csB = pp[1 + 2 * (e % 2)]
                        if e >= 2:
                            ep_ = e - 2
                            t.wait_ge(s_e, be + (ep_ // IG) * dve_sec + ((ep_ % IG) + 1) * 2)
                        t.matmul(csA[:, :], ctA[:, :], t0[:, ig * 512:(ig + 1) * 512],
                                 start=True, stop=False)
                        t.matmul(csA[:, :], ctB[:, :], t1[:, ig * 512:(ig + 1) * 512],
                                 start=False, stop=True)
                        t.matmul(csB[:, :], stA[:, :], t0[:, ig * 512:(ig + 1) * 512],
                                 start=True, stop=False)
                        t.matmul(csB[:, :], stB[:, :], t1[:, ig * 512:(ig + 1) * 512],
                                 start=False, stop=True).then_inc(s_m, 1)
                        for ks in range(ig * TPG, (ig + 1) * TPG):
                            tr_one(ks)
                else:
                    for ks in range(2 * DC):
                        tr_one(ks)

        @block.vector
        def _(v):
            def phase1(i):
                v.wait_ge(s_x, bx + 16 * (i + 1))
                P = 8 * (i % 2)
                op = v.reduce_sum(small[:, P:P + 1], xln[i % NXR][:, :], axis=AX)
                hop(v, "v", op)
                v.tensor_scalar(small[:, P + 1:P + 2], small[:, P:P + 1], 1.0 / DM,
                                None, ALU.mult).then_inc(s_a, 1)

            def phase2(j):
                P = 8 * (j % 2)
                v.wait_ge(s_b, bb + 2 * j + 1)
                v.wait_ge(s_a, ba + j + 1)
                op = v.tensor_tensor(small[:, P + 3:P + 4], small[:, P + 1:P + 2],
                                     small[:, P + 1:P + 2], ALU.mult)
                hop(v, "v", op)
                v.scalar_tensor_tensor(small[:, P + 4:P + 5], small[:, P + 2:P + 3],
                                       1.0 / DM, small[:, P + 3:P + 4],
                                       ALU.mult, ALU.subtract).then_inc(s_v, 1)
                v.wait_ge(s_b, bb + 2 * j + 2)
                op = v.reciprocal(small[:, P + 6:P + 7], small[:, P + 5:P + 6])
                hop(v, "v", op)
                v.scalar_tensor_tensor(small[:, P + 7:P + 8], small[:, P + 1:P + 2],
                                       -1.0, small[:, P + 6:P + 7],
                                       ALU.mult, ALU.mult).then_inc(s_h, 1)
                v.wait_ge(s_h, bh + j + 1)
                dst = xtok[((j // 2) % 2) * 2 + (j % 2)]
                v.tensor_scalar(dst[:, :], xln[j % NXR][:, :],
                                small[:, P + 6:P + 7], small[:, P + 7:P + 8],
                                ALU.mult, ALU.add).then_inc(s_n, 1)

            def sec_work(sct):
                if with_cs:
                    for ig in range(IG):
                        e = sct * IG + ig
                        csA = pp[0 + 2 * (e % 2)]
                        csB = pp[1 + 2 * (e % 2)]
                        v.wait_ge(s_m, msc_cs(sct, ig))
                        first = cs_first and sct == 0
                        if first:
                            v.tensor_copy(cs_stage[0][:, ig * 512:(ig + 1) * 512], csA[:, :])
                            v.tensor_copy(cs_stage[1][:, ig * 512:(ig + 1) * 512], csB[:, :]
                                          ).then_inc(s_e, 2)
                        else:
                            sl0 = cs_stage[0][:, ig * 512:(ig + 1) * 512]
                            sl1 = cs_stage[1][:, ig * 512:(ig + 1) * 512]
                            v.tensor_add(sl0, csA[:, :], sl0)
                            v.tensor_add(sl1, csB[:, :], sl1).then_inc(s_e, 2)
            for i in range(ntile + 1):
                if i < ntile:
                    phase1(i)
                if i >= 1:
                    phase2(i - 1)
                if i >= 2 and i % 2 == 0:
                    sec_work(i // 2 - 1)

        SEM_BASE["LNx"] = bx + 16 * ntile
        SEM_BASE["LNa"] = ba + ntile
        SEM_BASE["LNv"] = bv + ntile
        SEM_BASE["LNh"] = bh + ntile
        SEM_BASE["LNb"] = bb + 2 * ntile
        SEM_BASE["LNm"] = bm + nsec * pe_sec
        SEM_BASE["LNe"] = be + nsec * dve_sec
        SEM_BASE["LNe2"] = be2 + nsec * 2 * DC
        SEM_BASE["LNc"] = bc + 32 * ntile
        SEM_BASE["LNn"] = bn + ntile

    PJN = [0]

    def emit_proj(tag, kind, wsrc, xnT_src, TSPAN, ntg, crop, srop, wch, stag,
                  vtmp, ksb, store_fn, nheads):
        tag = "PJ"
        PJN[0] += 1
        pjid = PJN[0]
        s_w, bw = bsem(tag + "w")
        s_p, bp = bsem(tag + "p")
        s_r, br = bsem(tag + "r")
        s_cp, bcp = bsem(tag + "cp")
        s_ev, bev = bsem(tag + "ev")
        s_st, bst = bsem(tag + "st")
        pertg_ev = 1 if kind != "v" else 4
        NWR = len(wch)

        @block.gpsimd
        def _(g):
            if pair:
                hr = g.alloc_register(f"pjhr{pjid}")
                g.reg_load(hr, meta_sb[0:1, 1:2])
                g._half_reg = hr

            NSP = 4

            def ldw(h):
                if h >= NWR:
                    g.wait_ge(s_p, bp + (h - NWR + 1) * ntg)
                dsp = DC // NSP
                for sp in range(NSP):
                    g.dma_start(out=wch[h % NWR][:, sp * dsp * 128:(sp + 1) * dsp * 128]
                                .rearrange("p (dc f) -> p dc f", f=128),
                                in_=wsrc[sp * dsp:(sp + 1) * dsp, h, :, :]
                                .rearrange("dc p f -> p dc f")).then_inc(s_w, 16)
            for h0 in range(min(NWR, nheads)):
                ldw(h0)
            for h in range(nheads):
                if h + NWR < nheads:
                    ldw(h + NWR)
                g.wait_ge(s_ev, bev + (h + 1) * ntg * pertg_ev)
                store_fn(g, h)

        @block.tensor
        def _(t):
            ntot = nheads * ntg

            def post(ep):
                t.wait_ge(s_cp, bcp + ep + 1)
                if kind != "v":
                    if ep >= 1:
                        t.wait_ge(s_ev, bev + ep)
                    t.matmul(pp[4][:, :], rot_b[:, :], ksb[ep % 2][:, :],
                             start=True, stop=True).then_inc(s_r, 1)
                else:
                    for ci in range(4):
                        k = ep * 4 + ci
                        pbx = pbA if k % 2 == 0 else pbB
                        if k >= 2:
                            t.wait_ge(s_ev, bev + k - 1)
                        t.transpose(pbx[:, 0:128], vtmp[ep % 2][:, ci * 128:(ci + 1) * 128],
                                    idn_b[:, :]).then_inc(s_r, 1)

            for h in range(nheads):
                t.wait_ge(s_w, bw + 64 * (h + 1))
                for tg in range(ntg):
                    e = h * ntg + tg
                    psA = pp[2 + (e % 2)]
                    if e >= 2:
                        t.wait_ge(s_cp, bcp + e - 1)
                    last = None
                    for dci in range(DC):
                        last = t.matmul(psA[:, :], wch[h % NWR][:, dci * 128:(dci + 1) * 128],
                                        xnT_src[:, dci * TSPAN + tg * 512: dci * TSPAN + tg * 512 + 512],
                                        start=(dci == 0), stop=(dci == DC - 1))
                    last.then_inc(s_p, 1)
                    if e >= 1:
                        post(e - 1)
            post(ntot - 1)

        @block.vector
        def _(v):
            for h in range(nheads):
                for tg in range(ntg):
                    e = h * ntg + tg
                    psA = pp[2 + (e % 2)]
                    v.wait_ge(s_p, bp + e + 1)
                    if h > 0 and tg == 0:
                        v.wait_ge(s_st, bst + 16 * h)
                    if kind == "v":
                        if e >= 2:
                            v.wait_ge(s_r, br + 4 * (e - 1))
                        v.tensor_copy(vtmp[e % 2][:, :], psA[:, :]).then_inc(s_cp, 1)
                        for ci in range(4):
                            k = e * 4 + ci
                            pbx = pbA if k % 2 == 0 else pbB
                            v.wait_ge(s_r, br + k + 1)
                            v.tensor_copy(stag[:, tg * 512 + ci * 128: tg * 512 + ci * 128 + 128],
                                          pbx[:, 0:128]).then_inc(s_ev, 1)
                    else:
                        if e >= 2:
                            v.wait_ge(s_r, br + e - 1)
                        v.tensor_copy(ksb[e % 2][:, :], psA[:, :]).then_inc(s_cp, 1)
                        v.wait_ge(s_cp, bcp + e + 1)
                        v.wait_ge(s_r, br + e + 1)
                        cs_ = crop[:, tg * 512: tg * 512 + 512]
                        ss_ = srop[:, tg * 512: tg * 512 + 512]
                        if DBG["rope"]:
                            op1 = v.tensor_tensor(ksb[2][:, :], ksb[e % 2][:, :], cs_, ALU.mult)
                            op2 = v.tensor_tensor(ksb[3][:, :], pp[4][:, :], ss_, ALU.mult)
                            hop(v, "v", [op1, op2])
                            v.tensor_add(stag[:, tg * 512: tg * 512 + 512],
                                         ksb[2][:, :], ksb[3][:, :]).then_inc(s_ev, 1)
                        else:
                            v.tensor_copy(stag[:, tg * 512: tg * 512 + 512],
                                          ksb[e % 2][:, :]).then_inc(s_ev, 1)


        SEM_BASE["PJw"] = bw + 64 * nheads
        SEM_BASE["PJp"] = bp + nheads * ntg
        SEM_BASE["PJr"] = br + nheads * ntg * (1 if kind != "v" else 4)
        SEM_BASE["PJcp"] = bcp + nheads * ntg
        SEM_BASE["PJev"] = bev + nheads * ntg * pertg_ev
        SEM_BASE["PJst"] = bst + 16 * nheads

    p2 = contextlib.ExitStack()

    def sb2(name, shape, dt):
        return p2.enter_context(nc.sbuf_tensor(name, shape, dt))

    xln2 = [sb2(f"xln2_{i}", [128, DM], F32) for i in range(2)]
    xtok2 = [sb2(f"xtok2_{i}", [128, DM], BF16) for i in range(4)]
    cstile2 = [sb2(f"cst2_{i}", [128, 128], BF16) for i in range(8)]
    cs_stage = [sb2(f"css_{i}", [128, DM], F32) for i in range(2)]
    xnT2 = sb2("xnT2", [128, DC * QSEC], BF16)
    wch2 = [sb2(f"wch2_{i}", [128, DC * 128], BF16) for i in range(3)]
    kst2 = sb2("kst2", [128, QSEC], BF16)
    vst2 = sb2("vst2", [128, QSEC], BF16)
    vtmp2 = [sb2(f"vtmp2_{i}", [128, 512], BF16) for i in range(2)]
    ksb2 = [sb2(f"ksb2_{i}", [128, 512], BF16) for i in range(4)]
    crkv2 = sb2("crkv2", [128, QSEC], BF16)
    srkv2 = sb2("srkv2", [128, QSEC], BF16)

    s_p2rope = sem("p2rope")
    s_p2cs = sem("p2cs")
    NTGQ = QSEC // 512

    NQ_EMIT = NQ if p2_nq is None else p2_nq
    for q in range(NQ_EMIT):
        @block.sync
        def _(s, q=q):
            s.dma_start(out=crkv2[:, :],
                        in_=crope_kv[:, q * QSEC:(q + 1) * QSEC]).then_inc(s_p2rope, 16)
            s.dma_start(out=srkv2[:, :],
                        in_=srope_kv[:, q * QSEC:(q + 1) * QSEC]).then_inc(s_p2rope, 16)
            s.wait_ge(s_p2rope, 32 * (q + 1))

        emit_ln(f"L{q}", x_kv, q * QSEC, QSEC // 128, xtok2, xln2, xnT2,
                QSEC, 0, with_cs=True, cstile=cstile2, cs_stage=cs_stage,
                cs_first=(q == 0), cs_row0=q * QSEC, csrc=ct_kv, ssrc=st_kv)
        join()

        if q == NQ_EMIT - 1:
            @block.gpsimd
            def _(g):
                if pair:
                    hrc = g.alloc_register("cshr")
                    g.reg_load(hrc, meta_sb[0:1, 1:2])
                for ci in range(2):
                    if pair:
                        with g.If_cmp(hrc, 0, "IS_EQ"):
                            g.dma_start(out=csbuf[ci, :, :],
                                        in_=cs_stage[ci][:, :]).then_inc(s_p2cs, 16)
                        with g.Else():
                            g.dma_start(out=csbuf[2 + ci, :, :],
                                        in_=cs_stage[ci][:, :]).then_inc(s_p2cs, 16)
                    else:
                        g.dma_start(out=csbuf[ci, :, :],
                                    in_=cs_stage[ci][:, :]).then_inc(s_p2cs, 16)

        for kindw, wsrc_ in [(kk, {"k": kwT, "v": vwT}[kk]) for kk in p2_kinds]:
            tagq = f"P2{kindw}{q}"
            s_stq = sem("PJst")

            if kindw == "k":
                def store_fn(g, h, q=q, s_stq=s_stq):
                    def wr(hf, h=h, q=q, s_stq=s_stq):
                        g.dma_start(out=kbufs[hf][h, :, q * QSEC:(q + 1) * QSEC],
                                    in_=kst2[:, :]).then_inc(s_stq, 16)
                    if pair:
                        with g.If_cmp(g._half_reg, 0, "IS_EQ"):
                            wr(0)
                        with g.Else():
                            wr(1)
                    else:
                        wr(0)
            else:
                def store_fn(g, h, q=q, s_stq=s_stq):
                    tb0 = q * (QSEC // 128)
                    def wr(hf, h=h, tb0=tb0, s_stq=s_stq):
                        g.dma_start(out=vbufs[hf][h, :, tb0:tb0 + QSEC // 128, :],
                                    in_=vst2[:, :].rearrange("p (tb d) -> p tb d", d=128)
                                    ).then_inc(s_stq, 16)
                    if pair:
                        with g.If_cmp(g._half_reg, 0, "IS_EQ"):
                            wr(0)
                        with g.Else():
                            wr(1)
                    else:
                        wr(0)

            emit_proj(tagq, kindw, wsrc_, xnT2, QSEC, NTGQ, crkv2, srkv2,
                      wch2, kst2 if kindw == "k" else vst2,
                      vtmp2, ksb2, store_fn, H)
            join()

    @block.gpsimd
    def _(g):
        if p2_flag and p2_kinds:
            g.wait_ge(sem("PJst"), SEM_BASE["PJst"])
        if p2_flag:
            g.wait_ge(s_p2cs, 32)
        if pair and p2_flag:
            g.memset(ep_sb[0:1, :], 0).then_inc(sem("flg0"), 1)
            g.wait_ge(sem("flg0"), 1)
            g.dma_start(out=ep_sb[0:1, 0:1], in_=meta[0:1, 0:1]).then_inc(sem("flg1"), 16)
            g.wait_ge(sem("flg1"), 16)
            hrf = g.alloc_register("flghr")
            g.reg_load(hrf, meta_sb[0:1, 1:2])
            with g.If_cmp(hrf, 0, "IS_EQ"):
                g.dma_start(out=flv[0, :, :], in_=ep_sb[0:1, :]).then_inc(sem("flg"), 16)
            with g.Else():
                g.dma_start(out=flv[1, :, :], in_=ep_sb[0:1, :]).then_inc(sem("flg"), 16)
            g.wait_ge(sem("flg"), 16)

    p2.close()
    join()
    if stop_after <= 2:
        return finish_early()

    pxn = contextlib.ExitStack()
    xnTo = pxn.enter_context(nc.sbuf_tensor("xnTo", [128, DC * OWN], BF16))
    p01 = contextlib.ExitStack()

    def sb01(name, shape, dt):
        return p01.enter_context(nc.sbuf_tensor(name, shape, dt))

    xln1 = [sb01(f"xln1_{i}", [128, DM], F32) for i in range(3)]
    xtok1 = [sb01(f"xtok1_{i}", [128, DM], BF16) for i in range(4)]
    wch1 = [sb01(f"wch1_{i}", [128, DC * 128], BF16) for i in range(3)]
    qst1 = sb01("qst1", [128, OWN], BF16)
    ksb1 = [sb01(f"ksb1_{i}", [128, 512], BF16) for i in range(4)]
    cro1 = sb01("cro1", [128, OWN], BF16)
    sro1 = sb01("sro1", [128, OWN], BF16)

    @block.sync
    def _(s):
        s.dma_start(out=cro1[:, :], in_=crope_own).then_inc(sem("p1rope"), 16)
        s.dma_start(out=sro1[:, :], in_=srope_own).then_inc(sem("p1rope"), 16)
        s.wait_ge(sem("p1rope"), 32)

    emit_ln("LO", x_own, 0, OWN // 128, xtok1, xln1, xnTo, OWN, 0, with_cs=False)
    join()

    def store_q(g, h):
        g.dma_start(out=qbuf[h, :, :], in_=qst1[:, :]).then_inc(sem("PJst"), 16)

    emit_proj("P1q", "q", qwT, xnTo, OWN, NTGO, cro1, sro1, wch1,
              qst1, None, ksb1, store_q, H)
    join()
    p01.close()
    if stop_after <= 3:
        return finish_early()

    p3 = contextlib.ExitStack()

    def sb3(name, shape, dt):
        return p3.enter_context(nc.sbuf_tensor(name, shape, dt))

    gch = [sb3(f"gch_{i}", [128, DC * 64], BF16) for i in range(2)]
    uch = [sb3(f"uch_{i}", [128, DC * 64], BF16) for i in range(2)]
    dch = [sb3(f"dch_{i}", [128, FC * 64], BF16) for i in range(2)]
    h_sb = sb3("h_sb", [128, FC * 512], BF16)
    gate_t = [sb3(f"gate_t{i}", [128, 512], BF16) for i in range(2)]
    ffn_t = [sb3(f"ffn_t{i}", [128, 512], BF16) for i in range(2)]

    s3gl = sem("p3gl")
    s3gm = sem("p3gm")
    s3si = sem("p3si")
    s3hc = sem("p3hc")
    s3dl = sem("p3dl")
    s3dm = sem("p3dm")
    s3de = sem("p3de")
    s3fs = sem("p3fs")

    for tg in range(NTGO):
        b_gl, b_gm, b_si, b_hc = 64 * FC * tg, 4 * FC * tg, FC * tg, FC * tg
        b_dl, b_dm, b_de, b_fs = 32 * MC * tg, 2 * MC * tg, MC * tg, 16 * MC * tg

        @block.sync
        def _(s, tg=tg, b_gm=b_gm, b_dm=b_dm, b_de=b_de, b_fs=b_fs):
            for fc in range(FC):
                for hf in range(2):
                    d0 = hf * (DC // 2)
                    if fc >= 1:
                        s.wait_ge(s3gm, b_gm + 4 * (fc - 1) + hf + 1)
                    s.dma_start(out=gch[hf][:, :].rearrange("p (dc f) -> p dc f", f=128),
                                in_=gwT[fc, d0:d0 + DC // 2, :, :].rearrange("dc p f -> p dc f")
                                ).then_inc(s3gl, 16)
                for hf in range(2):
                    d0 = hf * (DC // 2)
                    if fc >= 1:
                        s.wait_ge(s3gm, b_gm + 4 * (fc - 1) + hf + 3)
                    s.dma_start(out=uch[hf][:, :].rearrange("p (dc f) -> p dc f", f=128),
                                in_=uwT[fc, d0:d0 + DC // 2, :, :].rearrange("dc p f -> p dc f")
                                ).then_inc(s3gl, 16)
            for mc in range(MC):
                for hf in range(2):
                    gh = 2 * mc + hf
                    if gh >= 2:
                        s.wait_ge(s3dm, b_dm + gh - 1)
                    f0 = hf * (FC // 2)
                    s.dma_start(out=dch[gh % 2][:, :].rearrange("p (fc f) -> p fc f", f=128),
                                in_=dwT[mc, f0:f0 + FC // 2, :, :].rearrange("fc p f -> p fc f")
                                ).then_inc(s3dl, 16)
                if mc >= 2:
                    s.wait_ge(s3de, b_de + mc - 1)
                    s.dma_start(out=ffnbuf[tg, mc - 2, :, :], in_=ffn_t[(mc - 2) % 2][:, :]
                                ).then_inc(s3fs, 16)
            for mc in range(MC - 2, MC):
                s.wait_ge(s3de, b_de + mc + 1)
                s.dma_start(out=ffnbuf[tg, mc, :, :], in_=ffn_t[mc % 2][:, :]
                            ).then_inc(s3fs, 16)

        @block.tensor
        def _(t, tg=tg, b_gl=b_gl, b_si=b_si, b_hc=b_hc, b_dl=b_dl, b_de=b_de):
            for fc in range(FC):
                psg = pp[2 + (fc % 2)]
                psu = pp[4 + (fc % 2)]
                if fc >= 2:
                    t.wait_ge(s3si, b_si + fc - 1)
                    t.wait_ge(s3hc, b_hc + fc - 1)
                for hf in range(2):
                    t.wait_ge(s3gl, b_gl + 64 * fc + 16 * (hf + 1))
                    last = None
                    for di in range(DC // 2):
                        dci = hf * (DC // 2) + di
                        last = t.matmul(psg[:, :], gch[hf][:, di * 128:(di + 1) * 128],
                                        xnTo[:, dci * OWN + tg * 512: dci * OWN + tg * 512 + 512],
                                        start=(dci == 0), stop=(dci == DC - 1))
                    last.then_inc(s3gm, 1)
                for hf in range(2):
                    t.wait_ge(s3gl, b_gl + 64 * fc + 16 * (hf + 3))
                    last = None
                    for di in range(DC // 2):
                        dci = hf * (DC // 2) + di
                        last = t.matmul(psu[:, :], uch[hf][:, di * 128:(di + 1) * 128],
                                        xnTo[:, dci * OWN + tg * 512: dci * OWN + tg * 512 + 512],
                                        start=(dci == 0), stop=(dci == DC - 1))
                    last.then_inc(s3gm, 1)
            for mc in range(MC):
                t.wait_ge(s3hc, b_hc + FC)
                psd = pp[2 + (mc % 2)]
                if mc >= 2:
                    t.wait_ge(s3de, b_de + mc - 1)
                for hf in range(2):
                    gh = 2 * mc + hf
                    t.wait_ge(s3dl, b_dl + 16 * (gh + 1))
                    last = None
                    for fi in range(FC // 2):
                        fc = hf * (FC // 2) + fi
                        last = t.matmul(psd[:, :], dch[gh % 2][:, fi * 128:(fi + 1) * 128],
                                        h_sb[:, fc * 512:(fc + 1) * 512],
                                        start=(fc == 0), stop=(fc == FC - 1))
                    last.then_inc(s3dm, 1)

        @block.scalar
        def _(a, tg=tg, b_gm=b_gm, b_dm=b_dm, b_hc=b_hc, b_fs=b_fs):
            for fc in range(FC):
                a.wait_ge(s3gm, b_gm + 4 * fc + 2)
                if fc >= 2:
                    a.wait_ge(s3hc, b_hc + fc - 1)
                psg = pp[2 + (fc % 2)]
                a.activation(gate_t[fc % 2][:, :], psg[:, :], AF.Silu,
                             scale=gs_sb[:, fc:fc + 1]).then_inc(s3si, 1)
            for mc in range(MC):
                a.wait_ge(s3dm, b_dm + 2 * (mc + 1))
                if mc >= 2:
                    a.wait_ge(s3fs, b_fs + 16 * (mc - 1))
                psd = pp[2 + (mc % 2)]
                a.activation(ffn_t[mc % 2][:, :], psd[:, :], AF.Copy,
                             scale=ds_sb[:, mc:mc + 1]).then_inc(s3de, 1)

        @block.vector
        def _(v, tg=tg, b_gm=b_gm, b_si=b_si):
            for fc in range(FC):
                v.wait_ge(s3gm, b_gm + 4 * fc + 4)
                v.wait_ge(s3si, b_si + fc + 1)
                psu = pp[4 + (fc % 2)]
                v.scalar_tensor_tensor(h_sb[:, fc * 512:(fc + 1) * 512], psu[:, :],
                                       us_sb[:, fc:fc + 1], gate_t[fc % 2][:, :],
                                       ALU.mult, ALU.mult).then_inc(s3hc, 1)


    @block.sync
    def _(s):
        s.wait_ge(s3fs, 16 * MC * NTGO)
    join()
    p3.close()
    pxn.close()
    if stop_after <= 4:
        return finish_early()

    p4 = contextlib.ExitStack()

    def sb4(name, shape, dt):
        return p4.enter_context(nc.sbuf_tensor(name, shape, dt))

    cs_a = sb4("cs_a", [128, DM], F32)
    cs_bq = sb4("cs_bq", [128, DM], F32)
    cs_d = sb4("cs_d", [128, DM], F32)
    C_sb = sb4("C_sb", [128, DM], BF16)
    S_sb = sb4("S_sb", [128, DM], BF16)
    wct = sb4("wct", [128, DM + 128], F32)
    wst = sb4("wst", [128, DM + 128], F32)
    wq4 = sb4("wq4", [128, DC * 512], BF16)
    dgm = [sb4(f"dgm{i}", [128, DM], BF16) for i in range(3)]
    comb4 = sb4("comb4", [128, DM], BF16)
    qrow = sb4("qrow", [1, DM], F32)
    qb_sb = sb4("qb_sb", [128, H], BF16)
    kvis4 = [sb4(f"kvis4_{i}", [128, NVIS], BF16) for i in range(2)]
    imp_row = sb4("imp_row", [1, NVIS], F32)
    imp_part = sb4("imp_part", [128, NVB], F32)
    imp_bc = sb4("imp_bc", [128, NVIS], F32)
    gtb = sb4("gtb", [128, NVIS], BF16)
    eqb = sb4("eqb", [128, NVIS], BF16)
    ltm_sb = sb4("ltm_sb", [128, NVB * NVIS], BF16)
    rrow = sb4("rrow", [1, NVIS], F32)
    mrank = sb4("mrank", [128, NVCOLB], F32)

    s4 = sem("p4chain")
    k4 = [0]
    d4 = {}

    def step(engf, fn):
        tgt = k4[0]

        def _f(e, tgt=tgt, fn=fn):
            if tgt > 0:
                e.wait_ge(s4, tgt)
            r = fn(e)
            if r is None:
                e.nop().then_inc(s4, 1)
            else:
                r.then_inc(s4, 1)
        engf(_f)
        k4[0] += 1

    def dstep(engf, fn):
        tgt = k4[0]
        key = engf.__func__.__name__ if hasattr(engf, "__func__") else engf.__name__
        if key not in d4:
            d4[key] = [sem(f"p4dma_{key}"), 0]
        sd = d4[key]
        sd[1] += 16
        dtgt = sd[1]

        def _f(e, tgt=tgt, fn=fn, dtgt=dtgt, sd=sd):
            if tgt > 0:
                e.wait_ge(s4, tgt)
            fn(e).then_inc(sd[0], 16)
            e.wait_ge(sd[0], dtgt)
            e.nop().then_inc(s4, 1)
        engf(_f)
        k4[0] += 1

    if pair:
        s4poll = sem("p4poll")

        def poll(g):
            epv = g.value_load(meta_sb[0:1, 0:1])
            cnt = g.alloc_register("p4cnt")
            f0 = g.alloc_register("p4f0")
            f1 = g.alloc_register("p4f1")
            g.reg_mov(cnt, 0)
            g.reg_mov(f0, 0)
            g.reg_mov(f1, 0)
            with g.While(lambda: (g.compute_val(f0) != epv) | (g.compute_val(f1) != epv)):
                g.reg_add(cnt, cnt, 16)
                g.dma_start(out=flag_sb[0:1, 0:32],
                            in_=flagbuf.rearrange("a b -> (a b)")[None, :]
                            ).then_inc(s4poll, 16)
                g.wait_ge(s4poll, cnt)
                g.reg_load(f0, flag_sb[0:1, 0:1])
                g.reg_load(f1, flag_sb[0:1, 16:17])
            return None
        step(block.gpsimd, poll)

    dstep(block.sync, lambda s: s.dma_start(out=cs_a[:, :], in_=csbuf[0, :, :]))
    if pair:
        dstep(block.sync, lambda s: s.dma_start(out=cs_bq[:, :], in_=csbuf[2, :, :]))
        step(block.vector, lambda v: v.tensor_add(cs_a[:, :], cs_a[:, :], cs_bq[:, :]))
    dstep(block.sync, lambda s: s.dma_start(out=cs_bq[:, :], in_=csbuf[1, :, :]))
    if pair:
        dstep(block.sync, lambda s: s.dma_start(out=cs_d[:, :], in_=csbuf[3, :, :]))
        step(block.vector, lambda v: v.tensor_add(cs_bq[:, :], cs_bq[:, :], cs_d[:, :]))

    for (srcm, dstm) in ((cs_a, C_sb), (cs_bq, S_sb)):
        for ic in range(DC):
            def tr2(t, srcm=srcm, ic=ic):
                return t.transpose(pp[5][:, 0:128], srcm[:, ic * 128:(ic + 1) * 128],
                                   idn_f[:, :])
            step(block.tensor, tr2)

            def ev(v, dstm=dstm, ic=ic):
                return v.tensor_copy(dstm[:, ic * 128:(ic + 1) * 128], pp[5][:, 0:128])
            step(block.vector, ev)

    dstep(block.sync, lambda s: s.dma_start(out=dgm[0][:, :], in_=dg1))
    dstep(block.sync, lambda s: s.dma_start(out=dgm[1][:, :], in_=dg2a))
    dstep(block.sync, lambda s: s.dma_start(out=dgm[2][:, :], in_=dg2b))
    step(block.vector, lambda v: v.memset(wst[:, 0:64], 0.0))
    step(block.vector, lambda v: v.memset(wst[:, 64 + DM:128 + DM], 0.0))

    for fg in range(IG):
        for hh in range(4):
            dstep(block.gpsimd,
                  lambda g, fg=fg, hh=hh: g.dma_start(
                      out=wq4[:, :].rearrange("p (dc g) -> p dc g", g=512)[:, :, hh * 128:(hh + 1) * 128],
                      in_=qwT[:, 4 * fg + hh, :, :].rearrange("dc p f -> p dc f")))

        def mms(t, fg=fg):
            last = None
            for ic in range(DC):
                t.matmul(pp[2][:, :], C_sb[:, ic * 128:(ic + 1) * 128],
                         wq4[:, ic * 512:(ic + 1) * 512],
                         start=(ic == 0), stop=(ic == DC - 1))
            for ic in range(DC):
                last = t.matmul(pp[3][:, :], S_sb[:, ic * 128:(ic + 1) * 128],
                                wq4[:, ic * 512:(ic + 1) * 512],
                                start=(ic == 0), stop=(ic == DC - 1))
            return last
        step(block.tensor, mms)

        def evw(v, fg=fg):
            v.tensor_copy(wct[:, 64 + fg * 512: 64 + (fg + 1) * 512], pp[2][:, :])
            return v.tensor_copy(wst[:, 64 + fg * 512: 64 + (fg + 1) * 512], pp[3][:, :])
        step(block.vector, evw)

    def qcomb(v):
        op1 = v.tensor_tensor(cs_a[:, :], dgm[0][:, :], wct[:, 64:64 + DM], ALU.mult)
        op2 = v.tensor_tensor(cs_bq[:, :], dgm[1][:, :], wst[:, 128:128 + DM], ALU.mult)
        hop(v, "v", [op1, op2])
        op3 = v.tensor_add(cs_a[:, :], cs_a[:, :], cs_bq[:, :])
        op4 = v.tensor_tensor(cs_bq[:, :], dgm[2][:, :], wst[:, 0:DM], ALU.mult)
        hop(v, "v", [op3, op4])
        return v.tensor_add(comb4[:, :], cs_a[:, :], cs_bq[:, :])
    step(block.vector, qcomb)

    for fg in range(IG):
        step(block.tensor,
             lambda t, fg=fg: t.matmul(pp[2][0:1, 0:512], ones_b[:, 0:1],
                                       comb4[:, fg * 512:(fg + 1) * 512],
                                       start=True, stop=True))
        step(block.vector,
             lambda v, fg=fg: v.tensor_copy(qrow[0:1, fg * 512:(fg + 1) * 512],
                                            pp[2][0:1, 0:512]))

    dstep(block.sync, lambda s: s.dma_start(out=qbarbuf, in_=qrow[0:1, :]))
    dstep(block.gpsimd, lambda g: g.dma_start(
        out=qb_sb[:, :], in_=qbarbuf.rearrange("a (h d) -> d (a h)", d=128),
        allow_slow_non_contiguous=True))

    for h in range(H):
        dstep(block.sync,
              lambda s, h=h: s.dma_start(out=kvis4[h % 2][:, :],
                                         in_=kbufs[0][h, :, VIS_LO:VIS_LO + NVIS]))
        step(block.tensor,
             lambda t, h=h: t.matmul(pp[3][0:1, 0:NVIS], qb_sb[:, h:h + 1],
                                     kvis4[h % 2][:, :],
                                     start=(h == 0), stop=(h == H - 1)))
    step(block.vector, lambda v: v.tensor_copy(imp_row[0:1, :], pp[3][0:1, 0:NVIS]))
    dstep(block.sync, lambda s: s.dma_start(out=impbuf, in_=imp_row[0:1, :]))
    dstep(block.sync, lambda s: s.dma_start(
        out=imp_part[:, :], in_=impbuf.rearrange("a (b p) -> p (a b)", p=128),
        allow_slow_non_contiguous=True))
    dstep(block.sync, lambda s: s.dma_start(out=ltm_sb[:, :], in_=ltm))

    step(block.tensor,
         lambda t: t.matmul(pp[4][:, 0:NVIS], ones1f[0:1, :], imp_row[0:1, :],
                            start=True, stop=True))
    step(block.vector, lambda v: v.tensor_copy(imp_bc[:, :], pp[4][:, 0:NVIS]))

    for ic2 in range(NVB):
        def rnk(v, ic2=ic2):
            op1 = v.tensor_scalar(gtb[:, :], imp_bc[:, :], imp_part[:, ic2:ic2 + 1], None, ALU.is_lt)
            op2 = v.tensor_scalar(eqb[:, :], imp_bc[:, :], imp_part[:, ic2:ic2 + 1], None, ALU.is_equal)
            hop(v, "v", [op1, op2])
            op3 = v.tensor_tensor(eqb[:, :], eqb[:, :], ltm_sb[:, ic2 * NVIS:(ic2 + 1) * NVIS], ALU.mult)
            hop(v, "v", op3)
            return v.tensor_add(gtb[:, :], gtb[:, :], eqb[:, :])
        step(block.vector, rnk)
        step(block.tensor,
             lambda t, ic2=ic2: t.matmul(pp[5][0:1, 0:NVIS], ones_b[:, 0:1], gtb[:, :],
                                         start=(ic2 == 0), stop=(ic2 == NVB - 1)))
    step(block.vector, lambda v: v.tensor_copy(rrow[0:1, :], pp[5][0:1, 0:NVIS]))
    dstep(block.sync, lambda s: s.dma_start(out=rankbuf, in_=rrow[0:1, :]))

    step(block.vector, lambda v: v.memset(mrank[:, :], 0.0))
    rb = rankbuf.rearrange("a j -> (a j)")
    p0_ = VIS_LO
    while p0_ < VIS_LO + NVIS:
        b = p0_ // 128
        pend = min((b + 1) * 128, VIS_LO + NVIS)

        def ldr(s, pa=p0_, pb_=pend, b=b):
            return s.dma_start(out=mrank[pa - b * 128: pb_ - b * 128, b:b + 1],
                               in_=rb[pa - VIS_LO:pb_ - VIS_LO].rearrange("(j x) -> j x", x=1))
        dstep(block.sync, ldr)
        p0_ = pend

    def mkmask(v):
        op0 = v.memset(maskcol[:, :], -float(M0))
        op1 = v.tensor_scalar(mrank[:, :], mrank[:, :], float(KEEP) - 0.5, None, ALU.is_ge)
        hop(v, "v", [op0, op1])
        return v.tensor_scalar(maskcol[:, 0:NVCOLB], mrank[:, :], -1e30, -float(M0),
                               ALU.mult, ALU.add)
    step(block.vector, mkmask)
    join()
    p4.close()
    if stop_after <= 5:
        return finish_early()

    attn_all = es.enter_context(nc.sbuf_tensor("attn_all", [128, H * OWN], BF16))
    p5 = contextlib.ExitStack()

    def sb5(name, shape, dt):
        return p5.enter_context(nc.sbuf_tensor(name, shape, dt))

    mst_sb = sb5("mst_sb", [128, 1008], BF16)
    kh5 = [sb5(f"kh5_{i}", [128, S], BF16) for i in range(2)]
    vh5 = [sb5(f"vh5_{i}", [128, S], BF16) for i in range(2)]
    qh5 = [sb5(f"qh5_{i}", [128, OWN], BF16) for i in range(2)]
    pt5 = [sb5(f"pt5_{i}", [128, 512], BF16) for i in range(4)]
    lrow = sb5("lrow", [1, 512], F32)
    rlbc = sb5("rlbc", [128, 512], F32)
    LOOK = 3

    s5ld = sem("p5ld")
    s5sm = sem("p5sm")
    s5ex = sem("p5ex")
    s5mk = sem("p5mk")
    s5pv = sem("p5pv")
    s5fin = sem("p5fin")
    s5fpe = sem("p5fpe")

    sched = []
    for g_ in range(NG):
        for j in range(32 * g_ + 32):
            sched.append((g_, j, 32 * g_ <= j, j == 32 * g_ + 31))
    PPH = len(sched)
    diag_pfx = [0]
    fin_pfx = [0]
    for (_, _, dg_, la_) in sched:
        diag_pfx.append(diag_pfx[-1] + (1 if dg_ else 0))
        fin_pfx.append(fin_pfx[-1] + (1 if la_ else 0))
    DPH = diag_pfx[-1]

    LDH = 16 * (2 * NHF + 1)

    @block.sync
    def _(s):
        s.dma_start(out=mst_sb[:, :], in_=mstar).then_inc(s5ld, 16)
        for h in range(H):
            if h >= 2:
                s.wait_ge(s5pv, (h - 1) * PPH)
            for hf in range(NHF):
                s.dma_start(out=kh5[h % 2][:, hf * HALF:(hf + 1) * HALF],
                            in_=kbufs[hf][h, :, :]).then_inc(s5ld, 16)
                s.dma_start(out=vh5[h % 2][:, hf * HALF:(hf + 1) * HALF],
                            in_=vbufs[hf][h, :, :, :].rearrange("p tb d -> p (tb d)")
                            ).then_inc(s5ld, 16)
            s.dma_start(out=qh5[h % 2][:, :], in_=qbuf[h, :, :]).then_inc(s5ld, 16)

    @block.tensor
    def _(t):
        for h in range(H):
            def s_pass(pi, h=h):
                g_, j, dg_, la_ = sched[pi]
                e = h * PPH + pi
                if e >= 4:
                    t.wait_ge(s5ex, e - 3)
                t.matmul(pp[e % 4][:, :], kh5[h % 2][:, j * 128:(j + 1) * 128],
                         qh5[h % 2][:, g_ * 512:(g_ + 1) * 512],
                         start=True, stop=True).then_inc(s5sm, 1)

            def pv_pass(pi, h=h):
                g_, j, dg_, la_ = sched[pi]
                e = h * PPH + pi
                fe = h * NG + g_
                first = (j == 0)
                if first:
                    t.wait_ge(s5fin, 3 * fe)
                if dg_:
                    t.wait_ge(s5mk, h * DPH + diag_pfx[pi + 1])
                else:
                    t.wait_ge(s5ex, e + 1)
                t.matmul(pp[4][:, :], vh5[h % 2][:, j * 128:(j + 1) * 128],
                         pt5[e % 4][:, :], start=first, stop=la_)
                t.matmul(pp[5][0:1, 0:512], ones_b[:, 0:1], pt5[e % 4][:, :],
                         start=first, stop=la_).then_inc(s5pv, 1)
                if la_:
                    t.wait_ge(s5fin, 3 * fe + 2)
                    t.matmul(pp[5][:, :], ones1f[0:1, :], lrow[0:1, :],
                             start=True, stop=True).then_inc(s5fpe, 1)

            t.wait_ge(s5ld, 16 + LDH * (h + 1))
            for pi in range(PPH + LOOK):
                if pi < PPH:
                    s_pass(pi)
                if pi >= LOOK:
                    pv_pass(pi - LOOK)

    @block.scalar
    def _(a):
        for h in range(H):
            for pi, (g_, j, dg_, la_) in enumerate(sched):
                e = h * PPH + pi
                a.wait_ge(s5sm, e + 1)
                if e >= 4:
                    a.wait_ge(s5pv, e - 3)
                a.activation(pt5[e % 4][:, :], pp[e % 4][:, :], AF.Exp,
                             scale=float(SCALE), bias=maskcol[:, j:j + 1]
                             ).then_inc(s5ex, 1)

    @block.vector
    def _(v):
        v.wait_ge(s5ld, 16)
        for h in range(H):
            for pi, (g_, j, dg_, la_) in enumerate(sched):
                e = h * PPH + pi
                if dg_:
                    v.wait_ge(s5ex, e + 1)
                    r = j - 32 * g_
                    u0 = 16 * (31 - r)
                    v.tensor_tensor(pt5[e % 4][:, :], pt5[e % 4][:, :],
                                    mst_sb[:, u0:u0 + 512], ALU.mult
                                    ).then_inc(s5mk, 1)
                if la_:
                    fe = h * NG + g_
                    v.wait_ge(s5pv, e + 1)
                    v.tensor_copy(lrow[0:1, :], pp[5][0:1, 0:512]).then_inc(s5fin, 1)
                    v.wait_ge(s5fin, 3 * fe + 1)
                    v.reciprocal(lrow[0:1, :], lrow[0:1, :]).then_inc(s5fin, 1)
                    v.wait_ge(s5fpe, fe + 1)
                    op = v.tensor_copy(rlbc[:, :], pp[5][:, :])
                    hop(v, "v", op)
                    v.tensor_tensor(attn_all[:, h * OWN + g_ * 512: h * OWN + (g_ + 1) * 512],
                                    pp[4][:, :], rlbc[:, :], ALU.mult
                                    ).then_inc(s5fin, 1)

    join()
    p5.close()
    if stop_after <= 6:
        return finish_early()

    p6 = contextlib.ExitStack()

    def sb6(name, shape, dt):
        return p6.enter_context(nc.sbuf_tensor(name, shape, dt))

    och = [sb6(f"och_{i}", [128, DC * 128], BF16) for i in range(2)]
    ffn6 = [sb6(f"ffn6_{i}", [128, 512], BF16) for i in range(2)]
    sum6 = [sb6(f"sum6_{i}", [128, 512], F32) for i in range(2)]
    xch6 = [sb6(f"xch6_{i}", [128, 512], F32) for i in range(2)]
    yt6 = [sb6(f"yt6_{i}", [128, 512], F32) for i in range(2)]

    s6ol = sem("p6ol")
    s6fl = sem("p6fl")
    s6xm = sem("p6xm")
    s6sm = sem("p6sm")
    s6tr = sem("p6tr")
    s6xl = sem("p6xl")
    s6ya = sem("p6ya")
    s6ys = sem("p6ys")
    NE6 = MC * NTGO

    @block.gpsimd
    def _(g):
        for mc in range(MC):
            if mc >= 2:
                g.wait_ge(s6xm, (mc - 1) * NTGO)
            g.dma_start(out=och[mc % 2][:, :].rearrange("p (dc f) -> p dc f", f=128),
                        in_=owT[:, mc, :, :].rearrange("dc p f -> p dc f")
                        ).then_inc(s6ol, 16)

    def ypo(e):
        mc2, tg2 = e // NTGO, e % NTGO
        return tg2 * 512, mc2 * 128

    @block.sync
    def _(s):
        for mc in range(MC):
            for tg in range(NTGO):
                e = mc * NTGO + tg
                if e >= 2:
                    s.wait_ge(s6sm, e - 1)
                s.dma_start(out=ffn6[e % 2][:, :], in_=ffnbuf[tg, mc, :, :]
                            ).then_inc(s6fl, 16)
                if e >= 1:
                    s.wait_ge(s6ya, 4 * e)
                    r0p, c0p = ypo(e - 1)
                    s.dma_start(out=y_out[r0p:r0p + 512, c0p:c0p + 128]
                                .rearrange("(ci p) d -> p ci d", p=128),
                                in_=yt6[(e - 1) % 2][:, :]
                                .rearrange("p (ci d) -> p ci d", d=128)
                                ).then_inc(s6ys, 16)
                r0 = tg * 512
                s.dma_start(out=xch6[e % 2][:, :].rearrange("p (ci d) -> p ci d", d=128),
                            in_=x_own[r0:r0 + 512, mc * 128:(mc + 1) * 128]
                            .rearrange("(ci p) d -> p ci d", p=128)
                            ).then_inc(s6xl, 16)
        s.wait_ge(s6ya, 4 * NE6)
        r0p, c0p = ypo(NE6 - 1)
        s.dma_start(out=y_out[r0p:r0p + 512, c0p:c0p + 128]
                    .rearrange("(ci p) d -> p ci d", p=128),
                    in_=yt6[(NE6 - 1) % 2][:, :]
                    .rearrange("p (ci d) -> p ci d", d=128)).then_inc(s6ys, 16)
        s.wait_ge(s6ys, 16 * NE6)

    @block.tensor
    def _(t):
        for mc in range(MC):
            t.wait_ge(s6ol, 16 * (mc + 1))
            for tg in range(NTGO):
                e = mc * NTGO + tg
                ps_om = pp[2 + (e % 2)]
                if e >= 2:
                    t.wait_ge(s6sm, e - 1)
                last = None
                for fc in range(DC):
                    last = t.matmul(ps_om[:, :], och[mc % 2][:, fc * 128:(fc + 1) * 128],
                                    attn_all[:, fc * OWN + tg * 512: fc * OWN + tg * 512 + 512],
                                    start=(fc == 0), stop=(fc == DC - 1))
                last.then_inc(s6xm, 1)
                t.wait_ge(s6sm, e + 1)
                for ci in range(4):
                    k = 4 * e + ci
                    pbx = pp[4 + (k % 2)]
                    if k >= 2:
                        t.wait_ge(s6ya, k - 1)
                    t.transpose(pbx[:, 0:128], sum6[e % 2][:, ci * 128:(ci + 1) * 128],
                                idn_f[:, :]).then_inc(s6tr, 1)

    @block.vector
    def _(v):
        for mc in range(MC):
            for tg in range(NTGO):
                e = mc * NTGO + tg
                ps_om = pp[2 + (e % 2)]
                v.wait_ge(s6xm, e + 1)
                v.wait_ge(s6fl, 16 * (e + 1))
                if e >= 2:
                    v.wait_ge(s6tr, 4 * (e - 1))
                v.tensor_add(sum6[e % 2][:, :], ps_om[:, :], ffn6[e % 2][:, :]
                             ).then_inc(s6sm, 1)
                v.wait_ge(s6xl, 16 * (e + 1))
                if e >= 2:
                    v.wait_ge(s6ys, 16 * (e - 1))
                for ci in range(4):
                    k = 4 * e + ci
                    pbx = pp[4 + (k % 2)]
                    v.wait_ge(s6tr, k + 1)
                    v.tensor_add(yt6[e % 2][:, ci * 128:(ci + 1) * 128], pbx[:, 0:128],
                                 xch6[e % 2][:, ci * 128:(ci + 1) * 128]
                                 ).then_inc(s6ya, 1)


    if debug_outs:
        s_dbg = sem("dbg")

        @block.sync
        def _(s):
            nd = 0
            for hh in range(2):
                if NHF == 2:
                    for hf in range(2):
                        s.dma_start(out=dbg_k[hh, :, hf * HALF:(hf + 1) * HALF],
                                    in_=kbufs[hf][hh, :, :]).then_inc(s_dbg, 16); nd += 1
                        s.dma_start(out=dbg_v[hh, :, hf * (HALF // 128):(hf + 1) * (HALF // 128), :],
                                    in_=vbufs[hf][hh, :, :, :]).then_inc(s_dbg, 16); nd += 1
                else:
                    s.dma_start(out=dbg_k[hh, :, :], in_=kbufs[0][hh, :, :]).then_inc(s_dbg, 16); nd += 1
                    s.dma_start(out=dbg_v[hh, :, :, :], in_=vbufs[0][hh, :, :, :]).then_inc(s_dbg, 16); nd += 1
                s.dma_start(out=dbg_q[hh, :, :], in_=qbuf[hh, :, :]).then_inc(s_dbg, 16); nd += 1
            s.dma_start(out=dbg_ffn, in_=ffnbuf).then_inc(s_dbg, 16); nd += 1
            s.dma_start(out=dbg_imp, in_=impbuf).then_inc(s_dbg, 16); nd += 1
            s.dma_start(out=dbg_rank, in_=rankbuf).then_inc(s_dbg, 16); nd += 1
            s.dma_start(out=dbg_qbar, in_=qbarbuf).then_inc(s_dbg, 16); nd += 1
            s.dma_start(out=dbg_cs, in_=csbuf).then_inc(s_dbg, 16); nd += 1
            s.dma_start(out=dbg_mask, in_=maskcol[:, :]).then_inc(s_dbg, 16); nd += 1
            s.dma_start(out=dbg_attn, in_=attn_all[:, :]).then_inc(s_dbg, 16); nd += 1
            s.wait_ge(s_dbg, 16 * nd)

    join()
    p6.close()
    es.close()
    return nc



def host_prepare(cfg, inputs, epoch):
    S, DM, H, DFF = cfg["S"], cfg["DM"], cfg["H"], cfg["DFF"]
    OWN, HALF = cfg["OWN"], cfg["HALF"]
    DC, FC, MC, NVB, NVIS = cfg["DC"], cfg["FC"], cfg["MC"], cfg["NVB"], cfg["NVIS"]
    pair = cfg["pair_mode"]

    x = np.ascontiguousarray(np.asarray(inputs["L0_attn_hidden_states"])[0])
    qw = np.asarray(inputs["L0_attn_q_w"])
    kw = np.asarray(inputs["L0_attn_k_w"])
    vw = np.asarray(inputs["L0_attn_v_w"])
    ow = np.asarray(inputs["L0_attn_o_w"])
    gw = np.asarray(inputs["L0_ffn_gate_w"])
    uw = np.asarray(inputs["L0_ffn_up_w"])
    dw = np.asarray(inputs["L0_ffn_down_w"])
    gs = np.asarray(inputs["L0_ffn_gate_s"]).astype(np.float32)
    us = np.asarray(inputs["L0_ffn_up_s"]).astype(np.float32)
    ds = np.asarray(inputs["L0_ffn_down_s"]).astype(np.float32)

    def tileT(w):
        o, i = w.shape
        return np.ascontiguousarray(
            w.T.reshape(i // 128, 128, o // 128, 128).transpose(0, 2, 1, 3))

    qwT = tileT(qw).astype(np.float32)
    kwT = tileT(kw).astype(np.float32)
    vwT = tileT(vw).astype(np.float32)
    owT = tileT(ow).astype(np.float32)
    gwT = np.ascontiguousarray(tileT(gw.astype(np.float32)).transpose(1, 0, 2, 3)).astype(BF)
    uwT = np.ascontiguousarray(tileT(uw.astype(np.float32)).transpose(1, 0, 2, 3)).astype(BF)
    dwT_ = np.ascontiguousarray(tileT(dw.astype(np.float32)).transpose(1, 0, 2, 3)).astype(BF)

    gs_t = np.ascontiguousarray(gs.reshape(FC, 128).T)
    us_t = np.ascontiguousarray(us.reshape(FC, 128).T)
    ds_t = np.ascontiguousarray(ds.reshape(MC, 128).T)

    inv_freq = 1.0 / 10000.0 ** (np.arange(0, 128, 2) / 128.0)

    def rope_tables(pos):
        ang = pos[None, :] * inv_freq[:, None]
        cr = np.concatenate([np.cos(ang), np.cos(ang)], 0).astype(BF)
        sr = np.concatenate([np.sin(ang), np.sin(ang)], 0).astype(BF)
        return np.ascontiguousarray(cr), np.ascontiguousarray(sr)

    dd = np.arange(128)[:, None]
    fm = np.arange(DM)[None, :] % 128
    dg1 = (dd == fm).astype(np.float32).astype(BF)
    dg2a = (-((fm < 64) & (dd == fm)).astype(np.float32)).astype(BF)
    dg2b = ((fm >= 64) & (dd == fm)).astype(np.float32).astype(BF)

    ltm = np.zeros((128, NVB * NVIS), BF)
    for ic in range(NVB):
        ii = ic * 128 + np.arange(128)[:, None]
        jj = np.arange(NVIS)[None, :]
        ltm[:, ic * NVIS:(ic + 1) * NVIS] = (ii < jj).astype(BF)

    idnf = np.eye(128, dtype=np.float32)
    idnb = np.eye(128, dtype=BF)
    rot = np.zeros((128, 128), np.float32)
    for d in range(64):
        rot[d + 64, d] = -1.0
    for d in range(64, 128):
        rot[d - 64, d] = 1.0
    rot = rot.astype(BF)

    iff = np.concatenate([inv_freq, inv_freq])

    in_maps = []
    for c in range(NC):
        half = c % 2 if pair else 0
        t0 = half * HALF if pair else 0
        pos_own = (np.arange(OWN) * 8 + c).astype(np.float64)
        cro, sro = rope_tables(pos_own)
        crk, srk = rope_tables(np.arange(t0, t0 + HALF).astype(np.float64))
        pos_kv = np.arange(t0, t0 + HALF)
        ang = pos_kv[:, None] * iff[None, :]
        msk = ((pos_kv >= cfg["TEXT_LO"]) & (pos_kv < cfg["TEXT_HI"]))[:, None]
        ctk = np.ascontiguousarray((np.cos(ang) * msk).astype(np.float32))
        stk = np.ascontiguousarray((np.sin(ang) * msk).astype(np.float32))
        kp = np.arange(128)[:, None]
        uu = np.arange(1008)[None, :]
        mstar = np.ascontiguousarray((kp <= 8 * uu - 3968 + c).astype(BF))
        mt = np.zeros((1, 16), np.int32)
        mt[0, 0] = epoch
        mt[0, 1] = half
        mt[0, 2] = c
        in_maps.append({
            "x_own": np.ascontiguousarray(x[c::8, :]),
            "x_kv": np.ascontiguousarray(x[t0:t0 + HALF, :]),
            "meta": mt,
            "qwT": qwT, "kwT": kwT, "vwT": vwT, "owT": owT,
            "gwT": gwT, "uwT": uwT, "dwT": dwT_,
            "gs_in": gs_t, "us_in": us_t, "ds_in": ds_t,
            "crope_own": cro, "srope_own": sro,
            "crope_kv": crk, "srope_kv": srk,
            "ct_kv": ctk, "st_kv": stk,
            "mstar": mstar,
            "dg1": dg1, "dg2a": dg2a, "dg2b": dg2b,
            "ltm": np.ascontiguousarray(ltm),
            "idnf_in": idnf, "idnb_in": idnb, "rot_in": np.ascontiguousarray(rot),
        })
    return in_maps


_PROG_CACHE = {}


def kernel(**inputs):
    cfg = make_cfg()
    if "full" not in _PROG_CACHE:
        _PROG_CACHE["full"] = build_program(cfg)
    nc = _PROG_CACHE["full"]
    import random
    epoch = random.randint(1, 2 ** 30)
    in_maps = host_prepare(cfg, inputs, epoch)
    res = run_bass_kernel_spmd(nc, in_maps, core_ids=list(range(NC)))
    S, DM = cfg["S"], cfg["DM"]
    y = np.empty((1, S, DM), np.float32)
    for c in range(NC):
        y[0, c::8, :] = res.results[c]["y_own"]
    return y
